# version 22
# baseline (speedup 1.0000x reference)
import numpy as np
import ml_dtypes

import concourse.bass as bass
import concourse.mybir as mybir
from concourse.bass import IndirectOffsetOnAxis
from concourse.bass_utils import run_bass_kernel_spmd

F32 = mybir.dt.float32
BF16 = mybir.dt.bfloat16
I32 = mybir.dt.int32
AF = mybir.ActivationFunctionType
ALU = mybir.AluOpType
AX = mybir.AxisListType.X

H = 2048
NT = 16
TC = 2048
NS = 16384
NE = 8
NA = 64
EPS = 1e-5
HEADS = 16


def _pack_lhsT(w):
    return np.ascontiguousarray(
        w.reshape(NT, 128, NT, 128).transpose(2, 1, 0, 3).reshape(NT, 128, H)
    ).astype(np.float32)


def _plan_slots(mask_idx):
    rows = (mask_idx // H).astype(np.int64)
    cols = (mask_idx % H).astype(np.int64)
    order = np.argsort(rows, kind="stable")
    counts = np.bincount(rows, minlength=H)
    row_start = np.zeros(H, dtype=np.int64)
    row_start[1:] = np.cumsum(counts)[:-1]
    rowperm = np.argsort(-counts, kind="stable")
    tiles_j, slot_s, slot_c = [], [], []
    sigma = np.full(H, H, dtype=np.int64)
    for pt in range(NT):
        rs = rowperm[pt * 128:(pt + 1) * 128]
        jmax = int(counts[rs].max())
        tiles_j.append(jmax)
        if jmax > 0:
            sigma[rs] = pt * 128 + np.arange(128)
        for j in range(jmax):
            s_ent = np.full(128, -1, dtype=np.int64)
            c_ent = np.zeros(128, dtype=np.int64)
            sel = counts[rs] > j
            s_ent[sel] = order[row_start[rs[sel]] + j]
            c_ent[sel] = cols[s_ent[sel]]
            slot_s.append(s_ent)
            slot_c.append(c_ent)
    return {"tiles_j": tiles_j,
            "slot_s": np.stack(slot_s),
            "slot_c": np.stack(slot_c),
            "sigma": sigma,
            "nst": len(slot_s)}


def build_program(tiles_j, nst):
    import contextlib
    nslot = nst * 128
    nc = bass.Bass("TRN2", target_bir_lowering=False)
    stack = contextlib.ExitStack()

    x_in = nc.dram_tensor("x", [TC, H], F32, kind="ExternalInput")
    y_out = nc.dram_tensor("y", [TC, H], F32, kind="ExternalOutput")
    w_in = {nm: nc.dram_tensor(nm, [NT, 128, H], F32, kind="ExternalInput")
            for nm in ("wq", "wv", "wu", "wo", "ww")}
    gwt_d = nc.dram_tensor("gwt", [128, NT * NE], F32, kind="ExternalInput")
    b_in = {nm: nc.dram_tensor(nm, [128, NT], F32, kind="ExternalInput")
            for nm in ("bqk", "bv", "bu", "bo", "lng", "lnb")}
    eaw_d = nc.dram_tensor("eaw", [NE, NA], F32, kind="ExternalInput")
    atoms_d = nc.dram_tensor("atoms_s", [NA, nslot], F32, kind="ExternalInput")
    imp_d = nc.dram_tensor("imp_s", [NE, nslot], F32, kind="ExternalInput")
    colidx_d = nc.dram_tensor("colidx", [128, nst], I32, kind="ExternalInput")
    sigma_d = nc.dram_tensor("sigma", [128, NT], I32, kind="ExternalInput")
    identb_d = nc.dram_tensor("ident_b", [128, 128], BF16, kind="ExternalInput")
    identf_d = nc.dram_tensor("ident_f", [128, 128], F32, kind="ExternalInput")

    wb = {nm: nc.dram_tensor(nm + "_b", [NT, 128, H], BF16) for nm in w_in}
    x_a = nc.dram_tensor("x_a", [TC, H], F32)
    x_b = nc.dram_tensor("x_b", [TC, H], F32)
    xn_hbm = nc.dram_tensor("xn_hbm", [TC, H], BF16)
    xnT_hbm = nc.dram_tensor("xnT_hbm", [H, TC], BF16)
    qT_hbm = nc.dram_tensor("qT_hbm", [H, TC], BF16)
    vT_hbm = nc.dram_tensor("vT_hbm", [H, TC], BF16)
    uT_hbm = nc.dram_tensor("uT_hbm", [H, TC], BF16)
    outT_hbm = nc.dram_tensor("outT_hbm", [H, TC], BF16)
    contrib_hbm = nc.dram_tensor("contrib_hbm", [H + 128, TC], BF16)
    cc_in = nc.dram_tensor("cc_in", [128, HEADS * 128], F32)
    cc_out = nc.dram_tensor("cc_out", [128, HEADS * 128], F32)
    ngrp = (nst + 15) // 16
    nslot_pad = ngrp * 2048
    weighted_hbm = nc.dram_tensor("weighted_hbm", [NE, nslot_pad], BF16)
    rows_d = nc.dram_tensor("rows_d", [32, 128], F32)

    sems = {}
    cnt = {}
    for nm in ("sd", "se", "gd", "cc", "t", "v", "c", "g",
               "x0", "x1", "x2", "x3", "wc", "e0", "e1",
               "w1", "w2", "w3", "w4", "w5", "tq", "sh", "pa"):
        sems[nm] = stack.enter_context(nc.semaphore(nm))
        cnt[nm] = 0

    def inc(nm, inst, k=1):
        if nm in ("sd", "se", "gd", "x0", "x1", "x2", "x3", "wc",
                  "e0", "e1", "w1", "w2", "w3", "w4", "w5", "tq", "sh"):
            k = 16
        inst.then_inc(sems[nm], k)
        cnt[nm] += k
        return cnt[nm]

    _uid = [0]

    def uname(p):
        _uid[0] += 1
        return f"{p}_{_uid[0]}"

    def sb(name, shape, dt):
        return stack.enter_context(nc.sbuf_tensor(name, shape, dt))

    XNT = sb("XNT", [128, NT * TC], BF16)
    wslab = [sb(f"wsl{i}", [128, H], BF16) for i in range(2)]
    sA = sb("sA", [128, H], BF16)
    sB = sb("sB", [128, H], BF16)
    sC = sb("sC", [128, H], BF16)
    sD = sb("sD", [128, H], BF16)
    sE = sb("sE", [128, H], BF16)
    sF = sb("sF", [128, H], BF16)
    sG = sb("sG", [128, H], BF16)
    fA = sb("fA", [128, H], F32)
    fB = sb("fB", [128, H], F32)
    kv_sb = sb("kv_sb", [128, HEADS * 128], F32)
    kvb = sb("kvb", [128, HEADS * 128], BF16)
    ident_b = sb("identb", [128, 128], BF16)
    ident_f = sb("identf", [128, 128], F32)
    gwt = sb("gwt_sb", [128, NT * NE], BF16)
    bias_sb = {nm: sb(nm + "_sb", [128, NT], F32) for nm in b_in}
    colidx = sb("colidx_sb", [128, nst], I32)
    sigma_sb = sb("sigma_sb", [128, NT], I32)
    sX = sb("sX", [128, H], BF16)
    sY = sb("sY", [128, H], BF16)
    wchunk = [sb(f"wch{i}", [NE, 2048], BF16) for i in range(2)]
    ones_k = sb("ones_k", [128, 1], BF16)
    epsv = sb("epsv", [128, 1], F32)
    ones_1 = sb("ones_1", [1, 128], BF16)
    statS = sb("statS", [128, NT], F32)
    statQ = sb("statQ", [128, NT], F32)
    statM = sb("statM", [128, NT], F32)
    statV = sb("statV", [128, NT], F32)
    statT = sb("statT", [128, NT], F32)
    statI = sb("statI", [128, NT], F32)
    lg_tm = sb("lg_tm", [128, NT * NE], F32)
    eq1 = sb("eq1", [128, NT * NE], F32)
    eq2 = sb("eq2", [128, NT * NE], F32)
    l2m = sb("l2m", [128, NT * NE], F32)
    p_tm = sb("p_tm", [128, NT * NE], F32)
    pb_tm = sb("pb_tm", [128, NT * NE], BF16)
    m1c = sb("m1c", [128, NT], F32)
    m2c = sb("m2c", [128, NT], F32)
    dc = sb("dc", [128, NT], F32)
    rw1c = sb("rw1c", [128, NT], F32)
    rw2c = sb("rw2c", [128, NT], F32)
    pT = sb("pT", [NE, TC], BF16)
    row_t = sb("row_t", [1, TC], F32)
    rowba = sb("rowba", [1, TC], BF16)
    rowbb = sb("rowbb", [1, TC], BF16)
    ones_f = sb("ones_f", [1, 128], F32)
    rows16 = sb("rows16", [16, 256], F32)

    def g3(t):
        return bass.AP(t, 0, [[NT * NE, 128], [NE, NT], [1, NE]])

    def g3b(t):
        return bass.AP(t, 0, [[NT, 128], [1, NT], [0, NE]])

    def xnt_rhs(k, n):
        return XNT[:, k * TC + n * 512: k * TC + (n + 1) * 512]

    cast_pending = []
    for nm, sm in (("ww", "w1"), ("wq", "w2"), ("wv", "w3"),
                   ("wu", "w4"), ("wo", "w5")):
        for piece in range(8):
            cast_pending.append((nm, sm, piece))
    cast_done = {}

    def emit_casts(e, k):
        for _ in range(k):
            if not cast_pending:
                return
            nm, sm, piece = cast_pending.pop(0)
            inc(sm, e.dma_start(
                out=wb[nm][piece * 2:piece * 2 + 2, :, :],
                in_=w_in[nm][piece * 2:piece * 2 + 2, :, :]))
            if piece == 7:
                cast_done[nm] = sm

    with nc.Block() as b:
        @b.gpsimd
        def _(e):
            inc("wc", e.dma_start(out=gwt[:, :], in_=gwt_d[:, :]))
            inc("g", e.memset(ones_k[:, :], 1.0))
            inc("g", e.memset(ones_1[:, :], 1.0))
            inc("g", e.memset(ones_f[:, :], 1.0))
            inc("g", e.memset(epsv[:, :], EPS))
            inc("g", e.memset(sG[:, :], 0.0))
            e.wait_ge(sems["g"], cnt["g"])
            inc("wc", e.dma_start(out=contrib_hbm[H:H + 128, :], in_=sG[:, :]))
            e.wait_ge(sems["wc"], cnt["wc"])

        @b.sync
        def _(e):
            inc("sd", e.dma_start(out=ident_b[:, :], in_=identb_d[:, :]), 16)
            inc("sd", e.dma_start(out=ident_f[:, :], in_=identf_d[:, :]), 16)
            for nm in b_in:
                inc("sd", e.dma_start(out=bias_sb[nm][:, :], in_=b_in[nm][:, :]), 16)
            inc("sd", e.dma_start(out=colidx[:, :], in_=colidx_d[:, :]), 16)
            inc("sd", e.dma_start(out=sigma_sb[:, :], in_=sigma_d[:, :]), 16)
            e.wait_ge(sems["sd"], cnt["sd"])

    eawT = sb("eawT", [NA, NE], BF16)
    with nc.Block() as b, \
         nc.sbuf_tensor("eaw_sb", [NE, NA], F32) as eaw_sb, \
         nc.sbuf_tensor("eawb", [NE, NA], BF16) as eawb, \
         nc.sbuf_tensor("erow", [NE, 4], F32) as erow, \
         nc.psum_tensor("ps_e", [NA, NE], BF16) as ps_e:
        g0, v0, c0, t0 = cnt["gd"], cnt["v"], cnt["c"], cnt["t"]

        @b.gpsimd
        def _(e):
            inc("gd", e.dma_start(out=eaw_sb[:, :], in_=eaw_d[:, :]))
            e.wait_ge(sems["gd"], cnt["gd"])

        @b.vector
        def _(e):
            e.wait_ge(sems["gd"], g0 + 16)
            inc("v", e.reduce_max(erow[:, 0:1], eaw_sb[:, :], axis=AX))
            e.wait_ge(sems["v"], v0 + 1)
            e.wait_ge(sems["c"], c0 + 2)
            inc("v", e.reduce_sum(erow[:, 2:3], eaw_sb[:, :], axis=AX))
            e.wait_ge(sems["v"], v0 + 2)
            inc("v", e.reciprocal(erow[:, 3:4], erow[:, 2:3]))
            e.wait_ge(sems["v"], v0 + 3)
            e.wait_ge(sems["t"], t0 + 1)
            inc("v", e.tensor_copy(eawT[:, :], ps_e[:, :]))
            e.wait_ge(sems["v"], v0 + 4)

        @b.scalar
        def _(e):
            e.wait_ge(sems["v"], v0 + 1)
            inc("c", e.mul(erow[:, 1:2], erow[:, 0:1], -1.0))
            e.wait_ge(sems["c"], c0 + 1)
            inc("c", e.activation(eaw_sb[:, :], eaw_sb[:, :], AF.Exp,
                                  bias=erow[:, 1:2], scale=1.0))
            e.wait_ge(sems["c"], c0 + 2)
            e.wait_ge(sems["v"], v0 + 3)
            inc("c", e.activation(eawb[:, :], eaw_sb[:, :], AF.Copy,
                                  bias=0.0, scale=erow[:, 3:4]))
            e.wait_ge(sems["c"], c0 + 3)

        @b.tensor
        def _(e):
            e.wait_ge(sems["c"], c0 + 3)
            inc("t", e.transpose(ps_e[:, :], eawb[:, :],
                                 ident_b[0:NE, 0:NE]))

    with nc.Block() as b, \
         nc.sbuf_tensor("atoms_ch", [NA, 4096], BF16) as atoms_ch, \
         nc.sbuf_tensor("imp_ch", [NE, 4096], F32) as imp_ch, \
         nc.sbuf_tensor("wtmp", [NE, 4096], BF16) as wtmp, \
         nc.psum_tensor("ps_d", [NE, 2048], F32) as ps_d:
        ndc = (nslot_pad + 2047) // 2048
        g0, v0, c0, t0 = cnt["gd"], cnt["v"], cnt["c"], cnt["t"]

        @b.gpsimd
        def _(e):
            for i in range(ndc):
                n0 = i * 2048
                pn = (i % 2) * 2048
                if i >= 2:
                    e.wait_ge(sems["v"], v0 + (i - 1))
                if n0 < nslot:
                    n1 = min(nslot, n0 + 2048)
                    inc("gd", e.dma_start(out=atoms_ch[:, pn:pn + n1 - n0],
                                          in_=atoms_d[:, n0:n1]))
                    inc("gd", e.dma_start(out=imp_ch[:, pn:pn + n1 - n0],
                                          in_=imp_d[:, n0:n1]))
                else:
                    inc("gd", e.dma_start(out=atoms_ch[:, pn:pn + 2048],
                                          in_=atoms_d[:, 0:2048]))
                    inc("gd", e.dma_start(out=imp_ch[:, pn:pn + 2048],
                                          in_=imp_d[:, 0:2048]))
                e.wait_ge(sems["gd"], cnt["gd"])
                if i >= 1:
                    e.wait_ge(sems["v"], v0 + i)
                    inc("gd", e.dma_start(
                        out=weighted_hbm[:, (i - 1) * 2048:i * 2048],
                        in_=wtmp[:, ((i - 1) % 2) * 2048:((i - 1) % 2) * 2048 + 2048]))
                    e.wait_ge(sems["gd"], cnt["gd"])
            e.wait_ge(sems["v"], v0 + ndc)
            inc("gd", e.dma_start(
                out=weighted_hbm[:, (ndc - 1) * 2048:ndc * 2048],
                in_=wtmp[:, ((ndc - 1) % 2) * 2048:((ndc - 1) % 2) * 2048 + 2048]))
            e.wait_ge(sems["gd"], cnt["gd"])

        @b.scalar
        def _(e):
            for i in range(ndc):
                pn = (i % 2) * 2048
                e.wait_ge(sems["gd"], g0 + 32 * (i + 1) + 16 * max(0, i))
                inc("c", e.activation(imp_ch[:, pn:pn + 2048],
                                      imp_ch[:, pn:pn + 2048], AF.Sigmoid,
                                      bias=0.0, scale=1.0))
                e.wait_ge(sems["c"], cnt["c"])

        @b.tensor
        def _(e):
            for i in range(ndc):
                pn = (i % 2) * 2048
                e.wait_ge(sems["gd"], g0 + 32 * (i + 1) + 16 * max(0, i))
                if i >= 1:
                    e.wait_ge(sems["v"], v0 + i)
                mm = None
                for q4 in range(4):
                    mm = e.matmul(ps_d[:, q4 * 512:(q4 + 1) * 512], eawT[:, :],
                                  atoms_ch[:, pn + q4 * 512:pn + (q4 + 1) * 512],
                                  start=True, stop=True)
                inc("t", mm)

        @b.vector
        def _(e):
            for i in range(ndc):
                pn = (i % 2) * 2048
                e.wait_ge(sems["t"], t0 + i + 1)
                e.wait_ge(sems["c"], c0 + i + 1)
                inc("v", e.tensor_tensor(wtmp[:, pn:pn + 2048], ps_d[:, :],
                                         imp_ch[:, pn:pn + 2048], ALU.mult))
                e.wait_ge(sems["v"], cnt["v"])

    def residual_ln_pass(x_src, addT, alpha, x_dst, want_xn, want_hbm=False):
        with nc.Block() as b, \
             nc.psum_tensor(uname("ptA"), [128, H], BF16) as ptA, \
             nc.psum_tensor(uname("ptB"), [128, H], BF16) as ptB, \
             nc.psum_tensor(uname("pb0"), [128, 512], F32) as pb0, \
             nc.psum_tensor(uname("pb1"), [128, 512], F32) as pb1:
            v0, c0, g0, gc0 = cnt["v"], cnt["c"], cnt["gd"], cnt["g"]
            t0, sd0, sh0 = cnt["t"], cnt["sd"], cnt["sh"]
            e0c = {nm: cnt[nm] for nm in ("e0", "e1")}
            store_x = addT is not None
            fAb = [fA, fB]
            stage = [sC, sD]
            pt = [ptA, ptB]
            ptr = [t.reshape([128, NT, 128]) for t in pt]
            XNT4 = XNT.reshape([128, NT, NT, 128])
            esem = ["e0", "e1"]
            rows_d2 = rows_d.reshape([2, TC])
            t_st = t0 + NT + 2
            v_ev0 = v0 + 3 * NT + 8
            v_ap0 = v0 + 3 * NT + 16

            def eload(e, j, src):
                inc(esem[j % 2], e.dma_start(
                    out=fAb[j % 2][:, :], in_=src[j * 128:(j + 1) * 128, :]))

            def ethr(j):
                return e0c[esem[j % 2]] + 16 * (j // 2 + 1)

            if addT is not None:
                @b.sync
                def _(e):
                    for j in range(NT):
                        if j >= 2:
                            e.wait_ge(sems["v"], v0 + 3 * (j - 2) + 1)
                        inc("sd", e.dma_start(
                            out=(sA if j % 2 == 0 else sB)[:, :],
                            in_=addT[:, j * 128:(j + 1) * 128],
                            transpose=True), 16)
                        e.wait_ge(sems["sd"], cnt["sd"])

            @b.scalar
            def _(e):
                for j in range(2):
                    eload(e, j, x_src)
                for j in range(NT):
                    e.wait_ge(sems["v"], v0 + 3 * j + 1)
                    if want_xn:
                        inc("c", e.activation(sG[:, :], fAb[j % 2][:, :],
                                              AF.Square, bias=0.0, scale=1.0,
                                              accum_out=statQ[:, j:j + 1]))
                    else:
                        inc("c", e.nop())
                    e.wait_ge(sems["c"], cnt["c"])
                    if want_xn and j >= 1:
                        e.wait_ge(sems["t"], t0 + j)
                        inc("g", e.activation(XNT4[:, :, j - 1, :],
                                              ptr[(j - 1) % 2][:, :, :],
                                              AF.Copy, bias=0.0, scale=1.0))
                        e.wait_ge(sems["g"], cnt["g"])
                    if j + 2 < NT:
                        e.wait_ge(sems["v"], v0 + 3 * (j + 1))
                        if store_x:
                            e.wait_ge(sems["gd"], g0 + 16 * (j + 1))
                        eload(e, j + 2, x_src)
                if want_xn:
                    e.wait_ge(sems["t"], t0 + NT)
                    inc("g", e.activation(XNT4[:, :, NT - 1, :],
                                          ptr[(NT - 1) % 2][:, :, :],
                                          AF.Copy, bias=0.0, scale=1.0))
                    e.wait_ge(sems["g"], cnt["g"])
                if want_xn:
                    e.wait_ge(sems["v"], v0 + 3 * NT + 4)
                    inc("c", e.activation(statT[:, :], statV[:, :], AF.Sqrt,
                                          bias=epsv[:, 0:1], scale=1.0))
                    e.wait_ge(sems["c"], cnt["c"])
                    e.wait_ge(sems["v"], v_ev0)
                    inc("sh", e.dma_start(out=rows_d[0:16, :],
                                          in_=rows16[0:16, 0:128]), 16)
                    inc("sh", e.dma_start(out=rows_d[16:32, :],
                                          in_=rows16[0:16, 128:256]), 16)
                    e.wait_ge(sems["sh"], sh0 + 32)
                    inc("sh", e.dma_start(out=row_t[0:1, :],
                                          in_=rows_d2[0:1, :]), 16)
                    inc("sh", e.dma_start(out=kv_sb[0:1, 0:TC],
                                          in_=rows_d2[1:2, :]), 16)
                    e.wait_ge(sems["sh"], sh0 + 64)

            @b.vector
            def _(e):
                for j in range(NT):
                    e.wait_ge(sems[esem[j % 2]], ethr(j))
                    if j >= 2:
                        e.wait_ge(sems["c"], c0 + (j - 1))
                        if store_x:
                            e.wait_ge(sems["gd"], g0 + 16 * (j - 1))
                    if addT is not None:
                        e.wait_ge(sems["sd"], sd0 + 16 * (j + 1))
                        inc("v", e.scalar_tensor_tensor(
                            fAb[j % 2][:, :],
                            (sA if j % 2 == 0 else sB)[:, :], float(alpha),
                            fAb[j % 2][:, :], ALU.mult, ALU.add))
                    else:
                        inc("v", e.nop())
                    e.wait_ge(sems["v"], cnt["v"])
                    if want_xn:
                        if j >= 2:
                            e.wait_ge(sems["t"], t0 + (j - 1))
                        inc("v", e.tensor_copy(stage[j % 2][:, :],
                                               fAb[j % 2][:, :]))
                        e.wait_ge(sems["v"], cnt["v"])
                        inc("v", e.reduce_sum(statS[:, j:j + 1],
                                              fAb[j % 2][:, :], axis=AX))
                    else:
                        inc("v", e.nop())
                        e.wait_ge(sems["v"], cnt["v"])
                        inc("v", e.nop())
                    e.wait_ge(sems["v"], cnt["v"])
                if want_xn:
                    e.wait_ge(sems["c"], c0 + NT)
                    inc("v", e.tensor_scalar_mul(statM[:, :], statS[:, :],
                                                 1.0 / H))
                    e.wait_ge(sems["v"], cnt["v"])
                    inc("v", e.tensor_scalar_mul(statV[:, :], statQ[:, :],
                                                 1.0 / H))
                    e.wait_ge(sems["v"], cnt["v"])
                    inc("v", e.tensor_mul(statT[:, :], statM[:, :],
                                          statM[:, :]))
                    e.wait_ge(sems["v"], cnt["v"])
                    inc("v", e.tensor_sub(statV[:, :], statV[:, :],
                                          statT[:, :]))
                    e.wait_ge(sems["v"], cnt["v"])
                    e.wait_ge(sems["c"], c0 + NT + 1)
                    inc("v", e.reciprocal(statI[:, :], statT[:, :]))
                    e.wait_ge(sems["v"], cnt["v"])
                    inc("v", e.tensor_mul(statT[:, :], statM[:, :],
                                          statI[:, :]))
                    e.wait_ge(sems["v"], cnt["v"])
                    inc("v", e.tensor_scalar_mul(statT[:, :], statT[:, :],
                                                 -1.0))
                    e.wait_ge(sems["v"], cnt["v"])
                    e.wait_ge(sems["t"], t_st)
                    inc("v", e.tensor_copy(rows16[:, :], pb0[0:16, 0:256]))
                    e.wait_ge(sems["v"], cnt["v"])
                    e.wait_ge(sems["c"], c0 + NT)
                    if store_x:
                        e.wait_ge(sems["gd"], g0 + 16 * NT)
                    for n in range(4):
                        e.wait_ge(sems["t"], t_st + 2 * (n + 1))
                        inc("v", e.tensor_copy(fA[:, n * 512:(n + 1) * 512],
                                               pb0[:, :]))
                        e.wait_ge(sems["v"], cnt["v"])
                        inc("v", e.tensor_copy(fB[:, n * 512:(n + 1) * 512],
                                               pb1[:, :]))
                        e.wait_ge(sems["v"], cnt["v"])
                    e.wait_ge(sems["g"], gc0 + NT)
                    pbs = [pb0, pb1]
                    for k in range(NT // 2):
                        for n in range(4):
                            c_lo = k * TC + n * 512
                            inc("v", e.tensor_mul(
                                pbs[n % 2][:, :], XNT[:, c_lo:c_lo + 512],
                                fA[:, n * 512:(n + 1) * 512]))
                            e.wait_ge(sems["v"], cnt["v"])
                            inc("v", e.tensor_add(
                                XNT[:, c_lo:c_lo + 512], pbs[n % 2][:, :],
                                fB[:, n * 512:(n + 1) * 512]))
                            e.wait_ge(sems["v"], cnt["v"])

            if want_xn:
                @b.tensor
                def _(e):
                    for j in range(NT):
                        e.wait_ge(sems["v"], v0 + 3 * j + 2)
                        if j >= 2:
                            e.wait_ge(sems["g"], gc0 + (j - 1))
                        tr = None
                        for k in range(NT):
                            tr = e.transpose(
                                pt[j % 2][:, k * 128:(k + 1) * 128],
                                stage[j % 2][:, k * 128:(k + 1) * 128],
                                ident_b[:, :])
                        inc("t", tr)
                    e.wait_ge(sems["v"], v0 + 3 * NT + 7)
                    inc("t", e.transpose(pb0[0:16, 0:128], statI[:, :],
                                         ident_f[:, :]))
                    inc("t", e.transpose(pb0[0:16, 128:256], statT[:, :],
                                         ident_f[:, :]))
                    e.wait_ge(sems["sh"], sh0 + 64)
                    for n in range(4):
                        if n >= 1:
                            e.wait_ge(sems["v"], v_ev0 + 2 * n)
                        inc("t", e.matmul(pb0[:, :], ones_f[:, :],
                                          row_t[0:1, n * 512:(n + 1) * 512],
                                          start=True, stop=True))
                        inc("t", e.matmul(pb1[:, :], ones_f[:, :],
                                          kv_sb[0:1, n * 512:(n + 1) * 512],
                                          start=True, stop=True))

            @b.gpsimd
            def _(e):
                for j in range(NT):
                    if j % 2 == 0:
                        emit_casts(e, 1)
                    if store_x:
                        e.wait_ge(sems["v"], v0 + 3 * j + 1)
                        inc("gd", e.dma_start(
                            out=x_dst[j * 128:(j + 1) * 128, :],
                            in_=fAb[j % 2][:, :]))
                        e.wait_ge(sems["gd"], cnt["gd"])
                if want_xn:
                    e.wait_ge(sems["g"], gc0 + NT)
                    e.wait_ge(sems["v"], v_ev0 + 8)
                    for k in range(NT // 2, NT):
                        for n in range(4):
                            c_lo = k * TC + n * 512
                            tmp = kv_sb[:, n * 512:(n + 1) * 512]
                            inc("pa", e.tensor_mul(
                                tmp, XNT[:, c_lo:c_lo + 512],
                                fA[:, n * 512:(n + 1) * 512]))
                            e.wait_ge(sems["pa"], cnt["pa"])
                            inc("pa", e.tensor_add(
                                XNT[:, c_lo:c_lo + 512], tmp,
                                fB[:, n * 512:(n + 1) * 512]))
                            e.wait_ge(sems["pa"], cnt["pa"])
                if want_hbm:
                    for k in range(NT):
                        if k < NT // 2:
                            e.wait_ge(sems["v"], v_ap0 + 8 * (k + 1))
                        inc("gd", e.dma_start(
                            out=xnT_hbm[k * 128:(k + 1) * 128, :],
                            in_=XNT[:, k * TC:(k + 1) * TC]))
                        e.wait_ge(sems["gd"], cnt["gd"])

    def matmul_phase(wdram_b, rhs_fn, evac_emit, cpi, wname=None):
        with nc.Block() as b, \
             nc.psum_tensor(uname("pA"), [128, 512], F32) as p0, \
             nc.psum_tensor(uname("pB"), [128, 512], F32) as p1, \
             nc.psum_tensor(uname("pC"), [128, 512], F32) as p2, \
             nc.psum_tensor(uname("pD"), [128, 512], F32) as p3, \
             nc.psum_tensor(uname("pE"), [128, 512], F32) as p4, \
             nc.psum_tensor(uname("pF"), [128, 512], F32) as p5, \
             nc.psum_tensor(uname("pG"), [128, 512], F32) as p6, \
             nc.psum_tensor(uname("pH"), [128, 512], F32) as p7:
            ps = [p0, p1, p2, p3, p4, p5, p6, p7]
            bases = {k: cnt[k] for k in cnt}

            wsem = cast_done.get(wname) if wname else None

            @b.sync
            def _(e):
                for m in range(NT):
                    if wsem:
                        e.wait_ge(sems[wsem], 128)
                    if m >= 2:
                        e.wait_ge(sems["t"], bases["t"] + (m - 1))
                    inc("sd", e.dma_start(out=wslab[m % 2][:, :],
                                          in_=wdram_b[m, :, :]), 16)
                    e.wait_ge(sems["sd"], cnt["sd"])

            @b.tensor
            def _(e):
                for m in range(NT):
                    e.wait_ge(sems["sd"], bases["sd"] + 16 * (m + 1))
                    if m >= 2:
                        e.wait_ge(sems["c"], bases["c"] + cpi * (m - 1))
                    mm = None
                    for k in range(NT):
                        for n in range(4):
                            mm = e.matmul(ps[4 * (m % 2) + n][:, :],
                                          wslab[m % 2][:, k * 128:(k + 1) * 128],
                                          rhs_fn(k, n),
                                          start=(k == 0), stop=(k == NT - 1))
                    inc("t", mm)

            evac_emit(b, ps, bases)
        return

    def evac_store(out_hbm, func, bias_name, elu):
        cpi = 5 if elu else 4

        def emit(b, ps, bases):
            tile = [sE, sF]

            @b.scalar
            def _(e):
                for m in range(NT):
                    e.wait_ge(sems["t"], bases["t"] + m + 1)
                    if m >= 2:
                        e.wait_ge(sems["gd"], bases["gd"] + 16 * (m - 1))
                    bias = bias_sb[bias_name][:, m:m + 1] if bias_name else 0.0
                    fn = func
                    if bias_name and fn == AF.Copy:
                        fn = AF.Identity
                    for n in range(4):
                        inc("c", e.activation(
                            tile[m % 2][:, n * 512:(n + 1) * 512],
                            ps[4 * (m % 2) + n][:, :], fn, bias=bias, scale=1.0))
                    if elu:
                        e.wait_ge(sems["v"], bases["v"] + 3 * m + 1)
                        inc("c", e.activation(sD[:, :], sC[:, :], AF.Exp,
                                              bias=0.0, scale=1.0))
                    e.wait_ge(sems["c"], cnt["c"])

            if elu:
                @b.vector
                def _(e):
                    for m in range(NT):
                        e.wait_ge(sems["c"], bases["c"] + cpi * m + 4)
                        inc("v", e.tensor_scalar_min(sC[:, :],
                                                     tile[m % 2][:, :], 0.0))
                        e.wait_ge(sems["v"], cnt["v"])
                        inc("v", e.tensor_scalar_max(sG[:, :],
                                                     tile[m % 2][:, :], 0.0))
                        e.wait_ge(sems["v"], cnt["v"])
                        e.wait_ge(sems["c"], bases["c"] + cpi * (m + 1))
                        inc("v", e.scalar_tensor_tensor(
                            tile[m % 2][:, :], sD[:, :], -1.0, sG[:, :],
                            ALU.add, ALU.add))
                        e.wait_ge(sems["v"], cnt["v"])

            @b.gpsimd
            def _(e):
                for m in range(NT):
                    if m % 4 == 0:
                        emit_casts(e, 1)
                    if elu:
                        e.wait_ge(sems["v"], bases["v"] + 3 * (m + 1))
                    else:
                        e.wait_ge(sems["c"], bases["c"] + cpi * (m + 1))
                    inc("gd", e.dma_start(
                        out=out_hbm[m * 128:(m + 1) * 128, :],
                        in_=tile[m % 2][:, :]))
                    e.wait_ge(sems["gd"], cnt["gd"])
        return emit, cpi

    def ffn(x_src, x_dst, next_want_xn):
        with nc.Block() as b, \
             nc.psum_tensor(uname("ps_l"), [NE, 512], F32) as ps_l:
            t0, v0 = cnt["t"], cnt["v"]

            @b.tensor
            def _(e):
                for n in range(4):
                    if n >= 1:
                        e.wait_ge(sems["v"], v0 + n)
                    mm = None
                    for k in range(NT):
                        mm = e.matmul(ps_l[:, :], gwt[:, k * NE:(k + 1) * NE],
                                      xnt_rhs(k, n),
                                      start=(k == 0), stop=(k == NT - 1))
                    inc("t", mm)

            @b.vector
            def _(e):
                for n in range(4):
                    e.wait_ge(sems["t"], t0 + n + 1)
                    inc("v", e.tensor_scalar(
                        fA[0:NE, n * 512:(n + 1) * 512], ps_l[:, :],
                        50.0, -50.0, ALU.min, ALU.max))
                    e.wait_ge(sems["v"], cnt["v"])

        with nc.Block() as b, \
             nc.psum_tensor(uname("ps_lt"), [128, NE], F32) as ps_lt:
            t0, v0 = cnt["t"], cnt["v"]

            @b.tensor
            def _(e):
                for j in range(NT):
                    if j >= 1:
                        e.wait_ge(sems["v"], v0 + j)
                    inc("t", e.transpose(ps_lt[:, :],
                                         fA[0:NE, j * 128:(j + 1) * 128],
                                         ident_f[0:NE, 0:NE]))

            @b.vector
            def _(e):
                for j in range(NT):
                    e.wait_ge(sems["t"], t0 + j + 1)
                    inc("v", e.tensor_copy(lg_tm[:, j * NE:(j + 1) * NE],
                                           ps_lt[:, :]))
                    e.wait_ge(sems["v"], cnt["v"])

        with nc.Block() as b, \
             nc.psum_tensor(uname("ps_pt"), [NE, 128], BF16) as ps_pt:
            t0, v0, c0 = cnt["t"], cnt["v"], cnt["c"]

            @b.vector
            def _(e):
                def lv(inst):
                    inc("v", inst)
                    e.wait_ge(sems["v"], cnt["v"])
                lv(e.reduce_max(m1c[:, :], g3(lg_tm), axis=AX))
                lv(e.tensor_tensor(g3(eq1), g3(lg_tm), g3b(m1c), ALU.is_equal))
                lv(e.scalar_tensor_tensor(g3(l2m), g3(eq1), -1e30, g3(lg_tm),
                                          ALU.mult, ALU.add))
                lv(e.reduce_max(m2c[:, :], g3(l2m), axis=AX))
                lv(e.tensor_tensor(g3(eq2), g3(l2m), g3b(m2c), ALU.is_equal))
                lv(e.tensor_sub(dc[:, :], m2c[:, :], m1c[:, :]))
                e.wait_ge(sems["c"], c0 + 1)
                lv(e.tensor_scalar_add(rw1c[:, :], dc[:, :], 1.0))
                lv(e.reciprocal(rw1c[:, :], rw1c[:, :]))
                lv(e.tensor_mul(rw2c[:, :], dc[:, :], rw1c[:, :]))
                lv(e.tensor_tensor(g3(p_tm), g3(eq1), g3b(rw1c), ALU.mult))
                lv(e.tensor_tensor(g3(eq2), g3(eq2), g3b(rw2c), ALU.mult))
                lv(e.tensor_add(p_tm[:, :], p_tm[:, :], eq2[:, :]))
                lv(e.tensor_copy(pb_tm[:, :], p_tm[:, :]))

            @b.scalar
            def _(e):
                e.wait_ge(sems["v"], v0 + 6)
                inc("c", e.activation(dc[:, :], dc[:, :], AF.Exp,
                                      bias=0.0, scale=1.0))
                e.wait_ge(sems["c"], cnt["c"])

            @b.tensor
            def _(e):
                e.wait_ge(sems["v"], v0 + 13)
                for j in range(NT):
                    if j >= 1:
                        e.wait_ge(sems["v"], v0 + 13 + j)
                    inc("t", e.transpose(ps_pt[:, :],
                                         pb_tm[:, j * NE:(j + 1) * NE],
                                         ident_b[:, :]))

            @b.vector
            def _(e):
                for j in range(NT):
                    e.wait_ge(sems["t"], t0 + j + 1)
                    inc("v", e.tensor_copy(pT[:, j * 128:(j + 1) * 128],
                                           ps_pt[:, :]))
                    e.wait_ge(sems["v"], cnt["v"])

        with nc.Block() as b, \
             nc.psum_tensor(uname("q0"), [128, 2048], F32) as q0_, \
             nc.psum_tensor(uname("q1"), [128, 2048], F32) as q1_:
            qs = [q0_, q1_]
            xg = [sA, sB, sX, sY]
            xsem = ["x0", "x1", "x2", "x3"]
            vt = [sC, sD]
            acc = [sE, sF]
            t0, v0, c0, s0, sd0 = (cnt["t"], cnt["v"], cnt["c"], cnt["gd"],
                                   cnt["sd"])
            x0c = {nm: cnt[nm] for nm in xsem}
            sched = []
            spt_of_pt = {}
            for pt in range(NT):
                if tiles_j[pt] == 0:
                    continue
                spt = len(spt_of_pt)
                spt_of_pt[pt] = spt
                for j in range(tiles_j[pt]):
                    sched.append((spt, pt, j))
            assert len(sched) == nst
            last_st = {}
            for i, (spt, pt, j) in enumerate(sched):
                last_st[spt] = i
            inv_last = {v: k for k, v in last_st.items()}
            pt_of_spt = {v: k for k, v in spt_of_pt.items()}
            ngrp_l = (nst + 15) // 16

            @b.sync
            def _(e):
                for g in range(ngrp_l):
                    if g >= 2:
                        e.wait_ge(sems["t"], t0 + 16 * (g - 1))
                    inc("sd", e.dma_start(
                        out=wchunk[g % 2][:, :],
                        in_=weighted_hbm[:, g * 2048:(g + 1) * 2048]), 16)
                    e.wait_ge(sems["sd"], cnt["sd"])

            @b.tensor
            def _(e):
                for st in range(nst):
                    g = st // 16
                    e.wait_ge(sems["sd"], sd0 + 16 * (g + 1))
                    if st >= 2:
                        e.wait_ge(sems["c"], c0 + (st - 1))
                    mm = None
                    for n in range(4):
                        mm = e.matmul(qs[st % 2][:, n * 512:(n + 1) * 512],
                                      wchunk[g % 2][:, (st % 16) * 128:
                                                    (st % 16 + 1) * 128],
                                      pT[:, n * 512:(n + 1) * 512],
                                      start=True, stop=True)
                    inc("t", mm)

            @b.scalar
            def _(e):
                for st, (spt, pt, j) in enumerate(sched):
                    e.wait_ge(sems["t"], t0 + st + 1)
                    if st >= 2:
                        e.wait_ge(sems["v"], v0 + 2 * (st - 1))
                    inc("c", e.activation(
                        vt[st % 2][:, :], qs[st % 2][:, :], AF.Copy,
                        bias=0.0, scale=1.0))
                    e.wait_ge(sems["c"], cnt["c"])

            @b.gpsimd
            def _(e):
                for st in range(nst):
                    if st % 12 == 0:
                        emit_casts(e, 1)
                    if st >= 4:
                        e.wait_ge(sems["v"], v0 + 2 * (st - 3))
                    inc(xsem[st % 4], e.indirect_dma_start(
                        xg[st % 4][:, :], None, xnT_hbm[:, :],
                        IndirectOffsetOnAxis(colidx[:, st:st + 1], 0)), 16)
                    if st in inv_last:
                        spt_l = inv_last[st]
                        e.wait_ge(sems["v"], v0 + 2 * (st + 1))
                        inc("gd", e.dma_start(
                            out=contrib_hbm[pt_of_spt[spt_l] * 128:
                                            (pt_of_spt[spt_l] + 1) * 128, :],
                            in_=acc[spt_l % 2][:, :]))
                        e.wait_ge(sems["gd"], cnt["gd"])

            @b.vector
            def _(e):
                for st, (spt, pt, j) in enumerate(sched):
                    e.wait_ge(sems[xsem[st % 4]],
                              x0c[xsem[st % 4]] + 16 * (st // 4 + 1))
                    e.wait_ge(sems["c"], c0 + (st + 1))
                    if j == 0:
                        if spt >= 2:
                            e.wait_ge(sems["gd"], s0 + 16 * (spt - 1))
                        inc("v", e.tensor_mul(acc[spt % 2][:, :],
                                              xg[st % 4][:, :],
                                              vt[st % 2][:, :]))
                        inc("v", e.engine_nop())
                        e.wait_ge(sems["v"], cnt["v"])
                    else:
                        inc("v", e.tensor_mul(wslab[st % 2][:, :],
                                              xg[st % 4][:, :],
                                              vt[st % 2][:, :]))
                        e.wait_ge(sems["v"], cnt["v"])
                        inc("v", e.tensor_add(acc[spt % 2][:, :],
                                              acc[spt % 2][:, :],
                                              wslab[st % 2][:, :]))
                        e.wait_ge(sems["v"], cnt["v"])


        def evac_w(b, ps, bases):
            tile = [sE, sF]
            cbuf = [sC, sD]

            @b.scalar
            def _(e):
                for m in range(NT):
                    e.wait_ge(sems["t"], bases["t"] + m + 1)
                    for n in range(4):
                        inc("c", e.activation(
                            tile[m % 2][:, n * 512:(n + 1) * 512],
                            ps[4 * (m % 2) + n][:, :], AF.Copy, bias=0.0, scale=1.0))
                    e.wait_ge(sems["c"], cnt["c"])
                    e.wait_ge(sems["v"], bases["v"] + (m + 1))
                    inc("se", e.dma_start(
                        out=outT_hbm[m * 128:(m + 1) * 128, :],
                        in_=tile[m % 2][:, :]))
                    e.wait_ge(sems["se"], cnt["se"])

            @b.gpsimd
            def _(e):
                for m in range(NT):
                    if m % 4 == 0:
                        emit_casts(e, 1)
                    if m >= 2:
                        e.wait_ge(sems["v"], bases["v"] + (m - 1))
                    inc("gd", e.indirect_dma_start(
                        cbuf[m % 2][:, :], None, contrib_hbm[:, :],
                        IndirectOffsetOnAxis(sigma_sb[:, m:m + 1], 0)))
                    e.wait_ge(sems["gd"], cnt["gd"])

            @b.vector
            def _(e):
                for m in range(NT):
                    e.wait_ge(sems["gd"], bases["gd"] + 16 * (m + 1))
                    e.wait_ge(sems["c"], bases["c"] + 4 * (m + 1))
                    inc("v", e.tensor_add(tile[m % 2][:, :], tile[m % 2][:, :],
                                          cbuf[m % 2][:, :]))
                    e.wait_ge(sems["v"], cnt["v"])


        matmul_phase(wb["ww"], xnt_rhs, evac_w, cpi=4, wname="ww")
        residual_ln_pass(x_src, outT_hbm, 0.5, x_dst,
                         want_xn=next_want_xn, want_hbm=False)

    def attention(x_src, x_dst):
        em, cpi = evac_store(qT_hbm, AF.Copy, "bqk", elu=True)
        matmul_phase(wb["wq"], xnt_rhs, em, cpi, wname="wq")
        em, cpi = evac_store(vT_hbm, AF.Silu, "bv", elu=False)
        matmul_phase(wb["wv"], xnt_rhs, em, cpi, wname="wv")

        with nc.Block() as b, \
             nc.psum_tensor(uname("pkv"), [128, HEADS * 128], F32) as pkv, \
             nc.psum_tensor(uname("u0"), [128, 512], F32) as u0_, \
             nc.psum_tensor(uname("u1"), [128, 512], F32) as u1_, \
             nc.psum_tensor(uname("u2"), [128, 512], F32) as u2_, \
             nc.psum_tensor(uname("u3"), [128, 512], F32) as u3_:
            us_ = [u0_, u1_, u2_, u3_]
            utile = [sE, sF]
            s0, t0, v0, c0, g0 = (cnt["sd"], cnt["t"], cnt["v"], cnt["c"],
                                  cnt["gd"])
            wsem_u = cast_done.get("wu")
            kvinc = {}
            utinc = {}
            pos = 0
            units_ku = []
            for m in range(NT):
                if m < 8:
                    for jj in (2 * m, 2 * m + 1):
                        pos += 1
                        kvinc[jj] = pos
                        units_ku.append(("kv", jj))
                pos += 1
                utinc[m] = pos
                units_ku.append(("uT", m))
            sd_q = {}
            sd_s = {}
            run = s0
            loads = []
            for u in units_ku:
                if u[0] == "kv":
                    loads.append(("q", u[1]))
                    loads.append(("v", u[1]))
                else:
                    loads.append(("s", u[1]))
            for kind, idx in loads:
                run += 16
                if kind == "v":
                    sd_q[idx] = run
                elif kind == "s":
                    sd_s[idx] = run

            @b.sync
            def _(e):
                for kind, idx in loads:
                    if kind == "q":
                        if idx >= 2:
                            e.wait_ge(sems["t"], t0 + kvinc[idx - 2])
                        inc("sd", e.dma_start(
                            out=(sA if idx % 2 == 0 else sC)[:, :],
                            in_=qT_hbm[:, idx * 128:(idx + 1) * 128],
                            transpose=True), 16)
                    elif kind == "v":
                        inc("sd", e.dma_start(
                            out=(sB if idx % 2 == 0 else sD)[:, :],
                            in_=vT_hbm[:, idx * 128:(idx + 1) * 128],
                            transpose=True), 16)
                    else:
                        if wsem_u:
                            e.wait_ge(sems[wsem_u], 128)
                        if idx >= 2:
                            e.wait_ge(sems["t"], t0 + utinc[idx - 2])
                        inc("sd", e.dma_start(out=wslab[idx % 2][:, :],
                                              in_=wb["wu"][idx, :, :]), 16)
                    e.wait_ge(sems["sd"], cnt["sd"])

            @b.tensor
            def _(e):
                for u in units_ku:
                    if u[0] == "kv":
                        j = u[1]
                        e.wait_ge(sems["sd"], sd_q[j])
                        if j >= 1:
                            e.wait_ge(sems["v"], v0 + j)
                        qa = sA if j % 2 == 0 else sC
                        va = sB if j % 2 == 0 else sD
                        mm = None
                        for h in range(HEADS):
                            mm = e.matmul(pkv[:, h * 128:(h + 1) * 128],
                                          qa[:, h * 128:(h + 1) * 128],
                                          va[:, h * 128:(h + 1) * 128],
                                          start=True, stop=True)
                        inc("t", mm)
                    else:
                        m = u[1]
                        e.wait_ge(sems["sd"], sd_s[m])
                        if m >= 1:
                            e.wait_ge(sems["c"], c0 + 4 * m)
                        mm = None
                        for k in range(NT):
                            for n in range(4):
                                mm = e.matmul(
                                    us_[n][:, :],
                                    wslab[m % 2][:, k * 128:(k + 1) * 128],
                                    xnt_rhs(k, n),
                                    start=(k == 0), stop=(k == NT - 1))
                        inc("t", mm)

            @b.vector
            def _(e):
                for j in range(NT):
                    e.wait_ge(sems["t"], t0 + kvinc[j])
                    if j == 0:
                        inc("v", e.tensor_copy(kv_sb[:, :], pkv[:, :]))
                    else:
                        inc("v", e.tensor_add(kv_sb[:, :], kv_sb[:, :],
                                              pkv[:, :]))
                    e.wait_ge(sems["v"], cnt["v"])

            @b.scalar
            def _(e):
                for m in range(NT):
                    e.wait_ge(sems["t"], t0 + utinc[m])
                    if m >= 2:
                        e.wait_ge(sems["gd"], g0 + 16 * (m - 1) +
                                  (16 if m - 2 >= 8 else 0))
                    for n in range(4):
                        inc("c", e.activation(
                            utile[m % 2][:, n * 512:(n + 1) * 512],
                            us_[n][:, :], AF.Silu,
                            bias=bias_sb["bu"][:, m:m + 1], scale=1.0))
                    e.wait_ge(sems["c"], cnt["c"])

            @b.gpsimd
            def _(e):
                for m in range(NT):
                    if m % 4 == 0:
                        emit_casts(e, 1)
                    e.wait_ge(sems["c"], c0 + 4 * (m + 1))
                    inc("gd", e.dma_start(
                        out=uT_hbm[m * 128:(m + 1) * 128, :],
                        in_=utile[m % 2][:, :]))
                    e.wait_ge(sems["gd"], cnt["gd"])
                    if m == 8:
                        e.wait_ge(sems["v"], v0 + NT)
                        inc("gd", e.dma_start(out=cc_in[:, :], in_=kv_sb[:, :]))
                        e.wait_ge(sems["gd"], cnt["gd"])
                        inc("cc", e.collective_compute(
                            "AllReduce", ALU.add,
                            replica_groups=[[0, 1, 2, 3], [4, 5, 6, 7]],
                            ins=[cc_in.ap().opt()],
                            outs=[cc_out.ap().opt()]))
        with nc.Block() as b, \
             nc.psum_tensor(uname("o0"), [128, 512], F32) as o0_, \
             nc.psum_tensor(uname("o1"), [128, 512], F32) as o1_, \
             nc.psum_tensor(uname("o2"), [128, 512], F32) as o2_, \
             nc.psum_tensor(uname("o3"), [128, 512], F32) as o3_:
            os_ = [o0_, o1_, o2_, o3_]
            s0, t0, c0, v0, g0 = (cnt["sd"], cnt["t"], cnt["c"], cnt["v"],
                                  cnt["gd"])

            @b.gpsimd
            def _(e):
                e.wait_ge(sems["cc"], cnt["cc"])
                inc("gd", e.dma_start(out=kv_sb[:, :], in_=cc_out[:, :]))
                e.wait_ge(sems["gd"], cnt["gd"])

            @b.vector
            def _(e):
                e.wait_ge(sems["gd"], g0 + 16)
                inc("v", e.tensor_copy(kvb[:, :], kv_sb[:, :]))
                e.wait_ge(sems["v"], cnt["v"])

            @b.sync
            def _(e):
                for h in range(HEADS):
                    if h >= 2:
                        e.wait_ge(sems["t"], t0 + (h - 1))
                    inc("sd", e.dma_start(
                        out=wslab[h % 2][:, :],
                        in_=qT_hbm[h * 128:(h + 1) * 128, :]), 16)
                    e.wait_ge(sems["sd"], cnt["sd"])

            @b.tensor
            def _(e):
                e.wait_ge(sems["v"], v0 + 1)
                for h in range(HEADS):
                    e.wait_ge(sems["sd"], s0 + 16 * (h + 1))
                    if h >= 1:
                        e.wait_ge(sems["c"], c0 + 4 * h)
                    mm = None
                    for n in range(4):
                        mm = e.matmul(os_[n][:, :],
                                      kvb[:, h * 128:(h + 1) * 128],
                                      wslab[h % 2][:, n * 512:(n + 1) * 512],
                                      start=True, stop=True)
                    inc("t", mm)

            @b.scalar
            def _(e):
                for h in range(HEADS):
                    e.wait_ge(sems["t"], t0 + h + 1)
                    for n in range(4):
                        inc("c", e.activation(
                            XNT[:, h * TC + n * 512:h * TC + (n + 1) * 512],
                            os_[n][:, :], AF.Copy, bias=0.0, scale=1.0))
                    e.wait_ge(sems["c"], cnt["c"])

        with nc.Block() as b, \
             nc.psum_tensor(uname("ps_r"), [1, TC], F32) as ps_r, \
             nc.psum_tensor(uname("ps_r2"), [1, TC], F32) as ps_r2:
            t0, c0, v0 = cnt["t"], cnt["c"], cnt["v"]

            @b.scalar
            def _(e):
                for h in range(HEADS):
                    if h >= 1:
                        e.wait_ge(sems["t"], t0 + h)
                    inc("c", e.activation(sG[:, :], XNT[:, h * TC:(h + 1) * TC],
                                          AF.Square, bias=0.0, scale=1.0))
                    e.wait_ge(sems["c"], cnt["c"])
                e.wait_ge(sems["v"], v0 + 4)
                inc("c", e.activation(row_t[0:1, 0:TC], fB[0:1, 0:TC], AF.Sqrt,
                                      bias=epsv[0:1, 0:1], scale=1.0))
                e.wait_ge(sems["c"], cnt["c"])

            @b.tensor
            def _(e):
                for h in range(HEADS):
                    e.wait_ge(sems["c"], c0 + h + 1)
                    mm = None
                    for n in range(4):
                        mm = e.matmul(ps_r[:, n * 512:(n + 1) * 512],
                                      ones_k[:, :],
                                      XNT[:, h * TC + n * 512:
                                          h * TC + (n + 1) * 512],
                                      start=(h == 0), stop=(h == HEADS - 1))
                    for n in range(4):
                        mm = e.matmul(ps_r2[:, n * 512:(n + 1) * 512],
                                      ones_k[:, :],
                                      sG[:, n * 512:(n + 1) * 512],
                                      start=(h == 0), stop=(h == HEADS - 1))
                    inc("t", mm)

            @b.vector
            def _(e):
                e.wait_ge(sems["t"], t0 + HEADS)
                inc("v", e.tensor_scalar_mul(fA[0:1, 0:TC], ps_r[:, :], 1.0 / H))
                inc("v", e.tensor_scalar_mul(fB[0:1, 0:TC], ps_r2[:, :], 1.0 / H))
                e.wait_ge(sems["v"], cnt["v"])
                inc("v", e.tensor_mul(row_t[0:1, 0:TC], fA[0:1, 0:TC],
                                      fA[0:1, 0:TC]))
                e.wait_ge(sems["v"], cnt["v"])
                inc("v", e.tensor_sub(fB[0:1, 0:TC], fB[0:1, 0:TC],
                                      row_t[0:1, 0:TC]))
                e.wait_ge(sems["v"], cnt["v"])
                e.wait_ge(sems["c"], c0 + HEADS + 1)
                inc("v", e.reciprocal(fB[0:1, 0:TC], row_t[0:1, 0:TC]))
                e.wait_ge(sems["v"], cnt["v"])
                inc("v", e.tensor_mul(row_t[0:1, 0:TC], fA[0:1, 0:TC],
                                      fB[0:1, 0:TC]))
                e.wait_ge(sems["v"], cnt["v"])
                inc("v", e.tensor_scalar_mul(row_t[0:1, 0:TC],
                                             row_t[0:1, 0:TC], -1.0))
                e.wait_ge(sems["v"], cnt["v"])
                inc("v", e.tensor_copy(rowba[:, :], fB[0:1, 0:TC]))
                inc("v", e.tensor_copy(rowbb[:, :], row_t[0:1, 0:TC]))
                e.wait_ge(sems["v"], cnt["v"])

        with nc.Block() as b, \
             nc.psum_tensor(uname("pb0"), [128, 512], F32) as pb0, \
             nc.psum_tensor(uname("pb1"), [128, 512], F32) as pb1:
            t0, v0, c0, s0 = cnt["t"], cnt["v"], cnt["c"], cnt["sd"]

            @b.tensor
            def _(e):
                for n in range(4):
                    if n >= 1:
                        e.wait_ge(sems["v"], v0 + 2 * n)
                    e.matmul(pb0[:, :], ones_1[:, :],
                             rowba[:, n * 512:(n + 1) * 512],
                             start=True, stop=True)
                    inc("t", e.matmul(pb1[:, :], ones_1[:, :],
                                      rowbb[:, n * 512:(n + 1) * 512],
                                      start=True, stop=True))

            @b.vector
            def _(e):
                for n in range(4):
                    e.wait_ge(sems["t"], t0 + n + 1)
                    inc("v", e.tensor_copy(sA[:, n * 512:(n + 1) * 512],
                                           pb0[:, :]))
                    e.wait_ge(sems["v"], cnt["v"])
                    inc("v", e.tensor_copy(sB[:, n * 512:(n + 1) * 512],
                                           pb1[:, :]))
                    e.wait_ge(sems["v"], cnt["v"])

            vb = v0 + 8

            @b.sync
            def _(e):
                for h in range(HEADS):
                    if h >= 2:
                        e.wait_ge(sems["v"], vb + 3 * (h - 1))
                    inc("sd", e.dma_start(
                        out=(sC if h % 2 == 0 else sD)[:, :],
                        in_=uT_hbm[h * 128:(h + 1) * 128, :]), 16)
                    e.wait_ge(sems["sd"], cnt["sd"])

            @b.vector
            def _(e):
                for h in range(HEADS):
                    e.wait_ge(sems["sd"], s0 + 16 * (h + 1))
                    inc("v", e.tensor_mul(sE[:, :], XNT[:, h * TC:(h + 1) * TC],
                                          sA[:, :]))
                    e.wait_ge(sems["v"], cnt["v"])
                    inc("v", e.tensor_add(sE[:, :], sE[:, :], sB[:, :]))
                    e.wait_ge(sems["v"], cnt["v"])
                    e.wait_ge(sems["c"], c0 + h + 1)
                    inc("v", e.tensor_mul(XNT[:, h * TC:(h + 1) * TC],
                                          sG[:, :],
                                          (sC if h % 2 == 0 else sD)[:, :]))
                    e.wait_ge(sems["v"], cnt["v"])

            @b.scalar
            def _(e):
                for h in range(HEADS):
                    e.wait_ge(sems["v"], vb + 3 * h + 2)
                    inc("c", e.activation(sG[:, :], sE[:, :], AF.Identity,
                                          bias=bias_sb["lnb"][:, h:h + 1],
                                          scale=bias_sb["lng"][:, h:h + 1]))
                    e.wait_ge(sems["c"], cnt["c"])

        em, cpi = evac_store(outT_hbm, AF.Copy, "bo", elu=False)
        matmul_phase(wb["wo"], xnt_rhs, em, cpi, wname="wo")
        residual_ln_pass(x_src, outT_hbm, 1.0, x_dst,
                         want_xn=True, want_hbm=True)

    residual_ln_pass(x_in, None, 0.0, x_a, want_xn=True, want_hbm=True)
    ffn(x_in, x_a, next_want_xn=True)
    attention(x_a, x_b)
    ffn(x_b, y_out, next_want_xn=False)

    with nc.Block() as b:
        @b.sync
        def _(e):
            e.wait_ge(sems["gd"], cnt["gd"])
            e.wait_ge(sems["sd"], cnt["sd"])
            e.wait_ge(sems["se"], cnt["se"])

    stack.close()
    return nc


def kernel(**inputs):
    x = np.asarray(inputs["L0_attn_x"], dtype=np.float32).reshape(-1, H)
    mask_idx = np.asarray(inputs["L0_ffn_mask_idx"]).astype(np.int64)
    plan = _plan_slots(mask_idx)
    nst = plan["nst"]
    nslot = nst * 128

    wq = _pack_lhsT(np.asarray(inputs["L0_attn_Wqk"], dtype=np.float32))
    wv = _pack_lhsT(np.asarray(inputs["L0_attn_Wv"], dtype=np.float32))
    wu = _pack_lhsT(np.asarray(inputs["L0_attn_Wu"], dtype=np.float32))
    wo = _pack_lhsT(np.asarray(inputs["L0_attn_Wo"], dtype=np.float32))
    ww = _pack_lhsT(np.ascontiguousarray(
        np.asarray(inputs["L0_ffn_W"], dtype=np.float32).T))

    gw = np.asarray(inputs["L0_ffn_gate_w"], dtype=np.float32)
    gwt = np.ascontiguousarray(
        gw.T.reshape(NT, 128, NE).transpose(1, 0, 2).reshape(128, NT * NE))

    def tile_cols(v):
        return np.ascontiguousarray(v.reshape(NT, 128).T).astype(np.float32)

    bqk = tile_cols(np.asarray(inputs["L0_attn_bqk"], dtype=np.float32))
    bv = tile_cols(np.asarray(inputs["L0_attn_bv"], dtype=np.float32))
    bu = tile_cols(np.asarray(inputs["L0_attn_bu"], dtype=np.float32))
    bo = tile_cols(np.asarray(inputs["L0_attn_bo"], dtype=np.float32))
    lng = tile_cols(np.asarray(inputs["L0_attn_ln_g"], dtype=np.float32))
    lnb = tile_cols(np.asarray(inputs["L0_attn_ln_b"], dtype=np.float32))

    atoms = np.asarray(inputs["L0_ffn_atoms"], dtype=np.float32)
    imp = np.asarray(inputs["L0_ffn_importance"], dtype=np.float32)
    slot_s = plan["slot_s"].reshape(-1)
    valid = slot_s >= 0
    atoms_s = np.zeros((NA, nslot), dtype=np.float32)
    atoms_s[:, valid] = atoms[:, slot_s[valid]]
    imp_s = np.zeros((NE, nslot), dtype=np.float32)
    imp_s[:, valid] = imp[:, slot_s[valid]]
    colidx = np.ascontiguousarray(plan["slot_c"].astype(np.int32).T)
    sigma = np.ascontiguousarray(
        plan["sigma"].astype(np.int32).reshape(NT, 128).T)

    eaw = np.asarray(inputs["L0_ffn_expert_atom_weights"], dtype=np.float32)
    ident_b = np.eye(128, dtype=np.float32).astype(ml_dtypes.bfloat16)
    ident_f = np.eye(128, dtype=np.float32)

    nc = build_program(plan["tiles_j"], nst)

    shared = {"wq": wq, "wv": wv, "wu": wu, "wo": wo, "ww": ww, "gwt": gwt,
              "bqk": bqk, "bv": bv, "bu": bu, "bo": bo, "lng": lng, "lnb": lnb,
              "eaw": eaw, "atoms_s": atoms_s, "imp_s": imp_s, "colidx": colidx,
              "sigma": sigma,
              "ident_b": ident_b, "ident_f": ident_f}
    in_maps = []
    for c in range(8):
        m = dict(shared)
        m["x"] = np.ascontiguousarray(x[c * TC:(c + 1) * TC])
        in_maps.append(m)

    res = run_bass_kernel_spmd(nc, in_maps, core_ids=list(range(8)))
    out = np.concatenate([res.results[c]["y"] for c in range(8)], axis=0)
    return out.reshape(2, 8192, H).astype(np.float32)


# version 23
# speedup vs baseline: 1.0365x; 1.0365x over previous
import numpy as np
import ml_dtypes

import concourse.bass as bass
import concourse.mybir as mybir
from concourse.bass import IndirectOffsetOnAxis
from concourse.bass_utils import run_bass_kernel_spmd

F32 = mybir.dt.float32
BF16 = mybir.dt.bfloat16
I32 = mybir.dt.int32
AF = mybir.ActivationFunctionType
ALU = mybir.AluOpType
AX = mybir.AxisListType.X

H = 2048
NT = 16
TC = 2048
NS = 16384
NE = 8
NA = 64
EPS = 1e-5
HEADS = 16


def _pack_lhsT(w):
    return np.ascontiguousarray(
        w.reshape(NT, 128, NT, 128).transpose(2, 1, 0, 3).reshape(NT, 128, H)
    ).astype(np.float32)


def _plan_slots(mask_idx):
    rows = (mask_idx // H).astype(np.int64)
    cols = (mask_idx % H).astype(np.int64)
    order = np.argsort(rows, kind="stable")
    counts = np.bincount(rows, minlength=H)
    row_start = np.zeros(H, dtype=np.int64)
    row_start[1:] = np.cumsum(counts)[:-1]
    rowperm = np.argsort(-counts, kind="stable")
    tiles_j, slot_s, slot_c = [], [], []
    sigma = np.full(H, H, dtype=np.int64)
    for pt in range(NT):
        rs = rowperm[pt * 128:(pt + 1) * 128]
        jmax = int(counts[rs].max())
        tiles_j.append(jmax)
        if jmax > 0:
            sigma[rs] = pt * 128 + np.arange(128)
        for j in range(jmax):
            s_ent = np.full(128, -1, dtype=np.int64)
            c_ent = np.zeros(128, dtype=np.int64)
            sel = counts[rs] > j
            s_ent[sel] = order[row_start[rs[sel]] + j]
            c_ent[sel] = cols[s_ent[sel]]
            slot_s.append(s_ent)
            slot_c.append(c_ent)
    return {"tiles_j": tiles_j,
            "slot_s": np.stack(slot_s),
            "slot_c": np.stack(slot_c),
            "sigma": sigma,
            "nst": len(slot_s)}


def build_program(tiles_j, nst):
    import contextlib
    nslot = nst * 128
    nc = bass.Bass("TRN2", target_bir_lowering=False)
    stack = contextlib.ExitStack()

    x_in = nc.dram_tensor("x", [TC, H], F32, kind="ExternalInput")
    y_out = nc.dram_tensor("y", [TC, H], F32, kind="ExternalOutput")
    w_in = {nm: nc.dram_tensor(nm, [NT, 128, H], F32, kind="ExternalInput")
            for nm in ("wq", "wv", "wu", "wo", "ww")}
    gwt_d = nc.dram_tensor("gwt", [128, NT * NE], F32, kind="ExternalInput")
    b_in = {nm: nc.dram_tensor(nm, [128, NT], F32, kind="ExternalInput")
            for nm in ("bqk", "bv", "bu", "bo", "lng", "lnb")}
    eaw_d = nc.dram_tensor("eaw", [NE, NA], F32, kind="ExternalInput")
    atoms_d = nc.dram_tensor("atoms_s", [NA, nslot], F32, kind="ExternalInput")
    imp_d = nc.dram_tensor("imp_s", [NE, nslot], F32, kind="ExternalInput")
    colidx_d = nc.dram_tensor("colidx", [128, nst], I32, kind="ExternalInput")
    sigma_d = nc.dram_tensor("sigma", [128, NT], I32, kind="ExternalInput")
    identb_d = nc.dram_tensor("ident_b", [128, 128], BF16, kind="ExternalInput")
    identf_d = nc.dram_tensor("ident_f", [128, 128], F32, kind="ExternalInput")

    wb = {nm: nc.dram_tensor(nm + "_b", [NT, 128, H], BF16) for nm in w_in}
    x_a = nc.dram_tensor("x_a", [TC, H], F32)
    x_b = nc.dram_tensor("x_b", [TC, H], F32)
    xn_hbm = nc.dram_tensor("xn_hbm", [TC, H], BF16)
    xnT_hbm = nc.dram_tensor("xnT_hbm", [H, TC], BF16)
    qT_hbm = nc.dram_tensor("qT_hbm", [H, TC], BF16)
    vT_hbm = nc.dram_tensor("vT_hbm", [H, TC], BF16)
    uT_hbm = nc.dram_tensor("uT_hbm", [H, TC], BF16)
    outT_hbm = nc.dram_tensor("outT_hbm", [H, TC], BF16)
    contrib_hbm = nc.dram_tensor("contrib_hbm", [H + 128, TC], BF16)
    cc_in = nc.dram_tensor("cc_in", [128, HEADS * 128], F32)
    cc_out = nc.dram_tensor("cc_out", [128, HEADS * 128], F32)
    ngrp = (nst + 15) // 16
    nslot_pad = ngrp * 2048
    weighted_hbm = nc.dram_tensor("weighted_hbm", [NE, nslot_pad], BF16)

    sems = {}
    cnt = {}
    for nm in ("sd", "se", "gd", "cc", "t", "v", "c", "g",
               "x0", "x1", "x2", "x3", "wc", "e0", "e1",
               "w1", "w2", "w3", "w4", "w5", "tq", "sh", "pa"):
        sems[nm] = stack.enter_context(nc.semaphore(nm))
        cnt[nm] = 0

    def inc(nm, inst, k=1):
        if nm in ("sd", "se", "gd", "x0", "x1", "x2", "x3", "wc",
                  "e0", "e1", "w1", "w2", "w3", "w4", "w5", "tq", "sh"):
            k = 16
        inst.then_inc(sems[nm], k)
        cnt[nm] += k
        return cnt[nm]

    _uid = [0]

    def uname(p):
        _uid[0] += 1
        return f"{p}_{_uid[0]}"

    def sb(name, shape, dt):
        return stack.enter_context(nc.sbuf_tensor(name, shape, dt))

    XNT = sb("XNT", [128, NT * TC], BF16)
    wslab = [sb(f"wsl{i}", [128, H], BF16) for i in range(2)]
    sA = sb("sA", [128, H], BF16)
    sB = sb("sB", [128, H], BF16)
    sC = sb("sC", [128, H], BF16)
    sD = sb("sD", [128, H], BF16)
    sE = sb("sE", [128, H], BF16)
    sF = sb("sF", [128, H], BF16)
    sG = sb("sG", [128, H], BF16)
    fA = sb("fA", [128, H], F32)
    fB = sb("fB", [128, H], F32)
    kv_sb = sb("kv_sb", [128, HEADS * 128], F32)
    kvb = sb("kvb", [128, HEADS * 128], BF16)
    ident_b = sb("identb", [128, 128], BF16)
    ident_f = sb("identf", [128, 128], F32)
    gwt = sb("gwt_sb", [128, NT * NE], BF16)
    bias_sb = {nm: sb(nm + "_sb", [128, NT], F32) for nm in b_in}
    colidx = sb("colidx_sb", [128, nst], I32)
    sigma_sb = sb("sigma_sb", [128, NT], I32)
    sX = sb("sX", [128, H], BF16)
    sY = sb("sY", [128, H], BF16)
    wchunk = [sb(f"wch{i}", [NE, 2048], BF16) for i in range(2)]
    ones_k = sb("ones_k", [128, 1], BF16)
    epsv = sb("epsv", [128, 1], F32)
    zerov = sb("zerov", [128, 1], F32)
    ones_1 = sb("ones_1", [1, 128], BF16)
    statS = sb("statS", [128, NT], F32)
    statQ = sb("statQ", [128, NT], F32)
    statM = sb("statM", [128, NT], F32)
    statV = sb("statV", [128, NT], F32)
    statT = sb("statT", [128, NT], F32)
    statI = sb("statI", [128, NT], F32)
    lg_tm = sb("lg_tm", [128, NT * NE], F32)
    eq1 = sb("eq1", [128, NT * NE], F32)
    eq2 = sb("eq2", [128, NT * NE], F32)
    l2m = sb("l2m", [128, NT * NE], F32)
    p_tm = sb("p_tm", [128, NT * NE], F32)
    pb_tm = sb("pb_tm", [128, NT * NE], BF16)
    m1c = sb("m1c", [128, NT], F32)
    m2c = sb("m2c", [128, NT], F32)
    dc = sb("dc", [128, NT], F32)
    rw1c = sb("rw1c", [128, NT], F32)
    rw2c = sb("rw2c", [128, NT], F32)
    pT = sb("pT", [NE, TC], BF16)
    row_t = sb("row_t", [1, TC], F32)
    rowba = sb("rowba", [1, TC], BF16)
    rowbb = sb("rowbb", [1, TC], BF16)

    def g3(t):
        return bass.AP(t, 0, [[NT * NE, 128], [NE, NT], [1, NE]])

    def g3b(t):
        return bass.AP(t, 0, [[NT, 128], [1, NT], [0, NE]])

    def xnt_rhs(k, n):
        return XNT[:, k * TC + n * 512: k * TC + (n + 1) * 512]

    cast_pending = []
    for nm, sm in (("ww", "w1"), ("wq", "w2"), ("wv", "w3"),
                   ("wu", "w4"), ("wo", "w5")):
        for piece in range(8):
            cast_pending.append((nm, sm, piece))
    cast_done = {}

    def emit_casts(e, k):
        for _ in range(k):
            if not cast_pending:
                return
            nm, sm, piece = cast_pending.pop(0)
            inc(sm, e.dma_start(
                out=wb[nm][piece * 2:piece * 2 + 2, :, :],
                in_=w_in[nm][piece * 2:piece * 2 + 2, :, :]))
            if piece == 7:
                cast_done[nm] = sm

    with nc.Block() as b:
        @b.gpsimd
        def _(e):
            inc("wc", e.dma_start(out=gwt[:, :], in_=gwt_d[:, :]))
            inc("g", e.memset(ones_k[:, :], 1.0))
            inc("g", e.memset(ones_1[:, :], 1.0))
            inc("g", e.memset(epsv[:, :], EPS))
            inc("g", e.memset(zerov[:, :], 0.0))
            inc("g", e.memset(sG[:, :], 0.0))
            e.wait_ge(sems["g"], cnt["g"])
            inc("wc", e.dma_start(out=contrib_hbm[H:H + 128, :], in_=sG[:, :]))
            e.wait_ge(sems["wc"], cnt["wc"])

        @b.sync
        def _(e):
            inc("sd", e.dma_start(out=ident_b[:, :], in_=identb_d[:, :]), 16)
            inc("sd", e.dma_start(out=ident_f[:, :], in_=identf_d[:, :]), 16)
            for nm in b_in:
                inc("sd", e.dma_start(out=bias_sb[nm][:, :], in_=b_in[nm][:, :]), 16)
            inc("sd", e.dma_start(out=colidx[:, :], in_=colidx_d[:, :]), 16)
            inc("sd", e.dma_start(out=sigma_sb[:, :], in_=sigma_d[:, :]), 16)
            e.wait_ge(sems["sd"], cnt["sd"])

    eawT = sb("eawT", [NA, NE], BF16)
    with nc.Block() as b, \
         nc.sbuf_tensor("eaw_sb", [NE, NA], F32) as eaw_sb, \
         nc.sbuf_tensor("eawb", [NE, NA], BF16) as eawb, \
         nc.sbuf_tensor("erow", [NE, 4], F32) as erow, \
         nc.psum_tensor("ps_e", [NA, NE], BF16) as ps_e:
        g0, v0, c0, t0 = cnt["gd"], cnt["v"], cnt["c"], cnt["t"]

        @b.gpsimd
        def _(e):
            inc("gd", e.dma_start(out=eaw_sb[:, :], in_=eaw_d[:, :]))
            e.wait_ge(sems["gd"], cnt["gd"])

        @b.vector
        def _(e):
            e.wait_ge(sems["gd"], g0 + 16)
            inc("v", e.reduce_max(erow[:, 0:1], eaw_sb[:, :], axis=AX))
            e.wait_ge(sems["v"], v0 + 1)
            e.wait_ge(sems["c"], c0 + 2)
            inc("v", e.reduce_sum(erow[:, 2:3], eaw_sb[:, :], axis=AX))
            e.wait_ge(sems["v"], v0 + 2)
            inc("v", e.reciprocal(erow[:, 3:4], erow[:, 2:3]))
            e.wait_ge(sems["v"], v0 + 3)
            e.wait_ge(sems["t"], t0 + 1)
            inc("v", e.tensor_copy(eawT[:, :], ps_e[:, :]))
            e.wait_ge(sems["v"], v0 + 4)

        @b.scalar
        def _(e):
            e.wait_ge(sems["v"], v0 + 1)
            inc("c", e.mul(erow[:, 1:2], erow[:, 0:1], -1.0))
            e.wait_ge(sems["c"], c0 + 1)
            inc("c", e.activation(eaw_sb[:, :], eaw_sb[:, :], AF.Exp,
                                  bias=erow[:, 1:2], scale=1.0))
            e.wait_ge(sems["c"], c0 + 2)
            e.wait_ge(sems["v"], v0 + 3)
            inc("c", e.activation(eawb[:, :], eaw_sb[:, :], AF.Copy,
                                  bias=0.0, scale=erow[:, 3:4]))
            e.wait_ge(sems["c"], c0 + 3)

        @b.tensor
        def _(e):
            e.wait_ge(sems["c"], c0 + 3)
            inc("t", e.transpose(ps_e[:, :], eawb[:, :],
                                 ident_b[0:NE, 0:NE]))

    with nc.Block() as b, \
         nc.sbuf_tensor("atoms_ch", [NA, 4096], BF16) as atoms_ch, \
         nc.sbuf_tensor("imp_ch", [NE, 4096], F32) as imp_ch, \
         nc.sbuf_tensor("wtmp", [NE, 4096], BF16) as wtmp, \
         nc.psum_tensor("ps_d", [NE, 2048], F32) as ps_d:
        ndc = (nslot_pad + 2047) // 2048
        g0, v0, c0, t0 = cnt["gd"], cnt["v"], cnt["c"], cnt["t"]

        @b.gpsimd
        def _(e):
            for i in range(ndc):
                n0 = i * 2048
                pn = (i % 2) * 2048
                if i >= 2:
                    e.wait_ge(sems["v"], v0 + (i - 1))
                if n0 < nslot:
                    n1 = min(nslot, n0 + 2048)
                    inc("gd", e.dma_start(out=atoms_ch[:, pn:pn + n1 - n0],
                                          in_=atoms_d[:, n0:n1]))
                    inc("gd", e.dma_start(out=imp_ch[:, pn:pn + n1 - n0],
                                          in_=imp_d[:, n0:n1]))
                else:
                    inc("gd", e.dma_start(out=atoms_ch[:, pn:pn + 2048],
                                          in_=atoms_d[:, 0:2048]))
                    inc("gd", e.dma_start(out=imp_ch[:, pn:pn + 2048],
                                          in_=imp_d[:, 0:2048]))
                e.wait_ge(sems["gd"], cnt["gd"])
                if i >= 1:
                    e.wait_ge(sems["v"], v0 + i)
                    inc("gd", e.dma_start(
                        out=weighted_hbm[:, (i - 1) * 2048:i * 2048],
                        in_=wtmp[:, ((i - 1) % 2) * 2048:((i - 1) % 2) * 2048 + 2048]))
                    e.wait_ge(sems["gd"], cnt["gd"])
            e.wait_ge(sems["v"], v0 + ndc)
            inc("gd", e.dma_start(
                out=weighted_hbm[:, (ndc - 1) * 2048:ndc * 2048],
                in_=wtmp[:, ((ndc - 1) % 2) * 2048:((ndc - 1) % 2) * 2048 + 2048]))
            e.wait_ge(sems["gd"], cnt["gd"])

        @b.scalar
        def _(e):
            for i in range(ndc):
                pn = (i % 2) * 2048
                e.wait_ge(sems["gd"], g0 + 32 * (i + 1) + 16 * max(0, i))
                inc("c", e.activation(imp_ch[:, pn:pn + 2048],
                                      imp_ch[:, pn:pn + 2048], AF.Sigmoid,
                                      bias=0.0, scale=1.0))
                e.wait_ge(sems["c"], cnt["c"])

        @b.tensor
        def _(e):
            for i in range(ndc):
                pn = (i % 2) * 2048
                e.wait_ge(sems["gd"], g0 + 32 * (i + 1) + 16 * max(0, i))
                if i >= 1:
                    e.wait_ge(sems["v"], v0 + i)
                mm = None
                for q4 in range(4):
                    mm = e.matmul(ps_d[:, q4 * 512:(q4 + 1) * 512], eawT[:, :],
                                  atoms_ch[:, pn + q4 * 512:pn + (q4 + 1) * 512],
                                  start=True, stop=True)
                inc("t", mm)

        @b.vector
        def _(e):
            for i in range(ndc):
                pn = (i % 2) * 2048
                e.wait_ge(sems["t"], t0 + i + 1)
                e.wait_ge(sems["c"], c0 + i + 1)
                inc("v", e.tensor_tensor(wtmp[:, pn:pn + 2048], ps_d[:, :],
                                         imp_ch[:, pn:pn + 2048], ALU.mult))
                e.wait_ge(sems["v"], cnt["v"])

    def residual_ln_pass(x_src, addT, alpha, x_dst, want_xn, want_hbm=False):
        with nc.Block() as b, \
             nc.psum_tensor(uname("ptA"), [128, H], BF16) as ptA, \
             nc.psum_tensor(uname("ptB"), [128, H], BF16) as ptB:
            v0, c0, g0, gc0 = cnt["v"], cnt["c"], cnt["gd"], cnt["g"]
            t0, sd0 = cnt["t"], cnt["sd"]
            e0c = {nm: cnt[nm] for nm in ("e0", "e1")}
            store_x = addT is not None
            fAb = [fA, fB]
            stage = [sC, sD]
            pt = [ptA, ptB]
            ptr = [t.reshape([128, NT, 128]) for t in pt]
            XNT4 = XNT.reshape([128, NT, NT, 128])
            esem = ["e0", "e1"]

            def xnT_slice(j):
                return bass.AP(xnT_hbm, j * 128,
                               [[TC, 128], [128 * TC, NT], [1, 128]])

            def eload(e, j, src):
                inc(esem[j % 2], e.dma_start(
                    out=fAb[j % 2][:, :], in_=src[j * 128:(j + 1) * 128, :]))

            def ethr(j):
                return e0c[esem[j % 2]] + 16 * (j // 2 + 1)

            v_combine = {}
            v_reduce = {}
            v_rm1 = {}
            v_recip = {}
            v_pcopy = {}
            c_square = {}
            c_sqrt = {}
            c_apply = {}
            g_pcopy = {}
            v = v0
            for j in range(NT):
                if addT is not None:
                    v += 1
                    v_combine[j] = v
                if want_xn:
                    v += 1
                    v_reduce[j] = v
                    v += 4
                    v_rm1[j] = v
                    if j >= 1:
                        v += 1
                        v_recip[j - 1] = v
                    if j >= 4 and (j - 3) % 2 == 1:
                        v += 1
                        v_pcopy[j - 3] = v
            if want_xn:
                v += 1
                v_recip[NT - 1] = v
                for k in (NT - 3, NT - 1):
                    v += 1
                    v_pcopy[k] = v
            c = c0
            g = gc0
            if want_xn:
                for j in range(NT):
                    c += 1
                    c_square[j] = c
                    c += 1
                    c_sqrt[j] = c
                    if j >= 1:
                        c += 2
                        c_apply[j - 1] = c
                    if j >= 3 and (j - 3) % 2 == 0:
                        g += 1
                        g_pcopy[j - 3] = g
                c += 2
                c_apply[NT - 1] = c
                g += 1
                g_pcopy[NT - 2] = g

            def t_trans(j):
                return t0 + j + 1

            def w_pcopy(e, j):
                if j % 2 == 0:
                    e.wait_ge(sems["g"], g_pcopy[j])
                else:
                    e.wait_ge(sems["v"], v_pcopy[j])

            sd_at = {}
            sd_store = {}
            sdv = sd0
            sync_ops = []
            for j in range(NT):
                if addT is not None:
                    sdv += 16
                    sd_at[j] = sdv
                    sync_ops.append(("at", j))
                if want_hbm and j >= 4:
                    sdv += 16
                    sd_store[j - 4] = sdv
                    sync_ops.append(("st", j - 4))
            if want_hbm:
                for k in range(NT - 4, NT):
                    sdv += 16
                    sd_store[k] = sdv
                    sync_ops.append(("st", k))

            if sync_ops:
                @b.sync
                def _(e):
                    for kind, j in sync_ops:
                        if kind == "at":
                            if j >= 2:
                                e.wait_ge(sems["v"], v_combine[j - 2])
                            inc("sd", e.dma_start(
                                out=(sA if j % 2 == 0 else sB)[:, :],
                                in_=addT[:, j * 128:(j + 1) * 128],
                                transpose=True), 16)
                        else:
                            w_pcopy(e, j)
                            inc("sd", e.dma_start(
                                out=xnT_slice(j),
                                in_=XNT4[:, :, j, :]), 16)
                        e.wait_ge(sems["sd"], cnt["sd"])

            def emit_napply(e, j):
                e.wait_ge(sems["v"], v_recip[j])
                inc("c", e.activation(statT[:, j:j + 1], statM[:, j:j + 1],
                                      AF.Identity, bias=zerov[:, 0:1],
                                      scale=statI[:, j:j + 1]))
                e.wait_ge(sems["c"], cnt["c"])
                if j >= 2:
                    e.wait_ge(sems["t"], t_trans(j - 2))
                inc("c", e.activation(stage[j % 2][:, :], fAb[j % 2][:, :],
                                      AF.Identity, bias=statT[:, j:j + 1],
                                      scale=statI[:, j:j + 1]))
                e.wait_ge(sems["c"], cnt["c"])

            def emit_pcopy_c(e, j):
                e.wait_ge(sems["t"], t_trans(j))
                inc("g", e.activation(XNT4[:, :, j, :],
                                      ptr[0][:, :, :], AF.Copy,
                                      bias=0.0, scale=1.0))
                e.wait_ge(sems["g"], cnt["g"])

            @b.scalar
            def _(e):
                for j in range(2):
                    eload(e, j, x_src)
                for j in range(NT):
                    if want_xn:
                        if addT is not None:
                            e.wait_ge(sems["v"], v_combine[j])
                        else:
                            e.wait_ge(sems[esem[j % 2]], ethr(j))
                        inc("c", e.activation(sG[:, :], fAb[j % 2][:, :],
                                              AF.Square, bias=0.0, scale=1.0,
                                              accum_out=statQ[:, j:j + 1]))
                        e.wait_ge(sems["c"], cnt["c"])
                        e.wait_ge(sems["v"], v_rm1[j])
                        inc("c", e.activation(statT[:, j:j + 1],
                                              statV[:, j:j + 1], AF.Sqrt,
                                              bias=epsv[:, 0:1], scale=1.0))
                        e.wait_ge(sems["c"], cnt["c"])
                        if j >= 1:
                            emit_napply(e, j - 1)
                        if j >= 3 and (j - 3) % 2 == 0:
                            emit_pcopy_c(e, j - 3)
                        if 1 <= j < NT - 1:
                            e.wait_ge(sems["v"], v_reduce[j - 1])
                            if store_x:
                                e.wait_ge(sems["gd"], g0 + 16 * j)
                            eload(e, j + 1, x_src)
                    else:
                        if j + 2 < NT:
                            if addT is not None:
                                e.wait_ge(sems["v"], v_combine[j])
                            else:
                                e.wait_ge(sems[esem[j % 2]], ethr(j))
                            if store_x:
                                e.wait_ge(sems["gd"], g0 + 16 * (j + 1))
                            eload(e, j + 2, x_src)
                if want_xn:
                    emit_napply(e, NT - 1)
                    emit_pcopy_c(e, NT - 2)

            @b.vector
            def _(e):
                for j in range(NT):
                    e.wait_ge(sems[esem[j % 2]], ethr(j))
                    if addT is not None:
                        if j >= 2:
                            if want_xn:
                                e.wait_ge(sems["c"], c_apply[j - 2])
                            if store_x:
                                e.wait_ge(sems["gd"], g0 + 16 * (j - 1))
                        e.wait_ge(sems["sd"], sd_at[j])
                        inc("v", e.scalar_tensor_tensor(
                            fAb[j % 2][:, :],
                            (sA if j % 2 == 0 else sB)[:, :], float(alpha),
                            fAb[j % 2][:, :], ALU.mult, ALU.add))
                        e.wait_ge(sems["v"], cnt["v"])
                    if want_xn:
                        inc("v", e.reduce_sum(statS[:, j:j + 1],
                                              fAb[j % 2][:, :], axis=AX))
                        e.wait_ge(sems["v"], cnt["v"])
                        inc("v", e.tensor_scalar_mul(statM[:, j:j + 1],
                                                     statS[:, j:j + 1],
                                                     -1.0 / H))
                        e.wait_ge(sems["c"], c_square[j])
                        inc("v", e.tensor_scalar_mul(statV[:, j:j + 1],
                                                     statQ[:, j:j + 1],
                                                     1.0 / H))
                        e.wait_ge(sems["v"], cnt["v"])
                        inc("v", e.tensor_mul(statT[:, j:j + 1],
                                              statM[:, j:j + 1],
                                              statM[:, j:j + 1]))
                        e.wait_ge(sems["v"], cnt["v"])
                        inc("v", e.tensor_sub(statV[:, j:j + 1],
                                              statV[:, j:j + 1],
                                              statT[:, j:j + 1]))
                        e.wait_ge(sems["v"], cnt["v"])
                        if j >= 1:
                            e.wait_ge(sems["c"], c_sqrt[j - 1])
                            inc("v", e.reciprocal(statI[:, j - 1:j],
                                                  statT[:, j - 1:j]))
                            e.wait_ge(sems["v"], cnt["v"])
                        if j >= 4 and (j - 3) % 2 == 1:
                            e.wait_ge(sems["t"], t_trans(j - 3))
                            inc("v", e.tensor_copy(XNT4[:, :, j - 3, :],
                                                   ptr[1][:, :, :]))
                            e.wait_ge(sems["v"], cnt["v"])
                if want_xn:
                    e.wait_ge(sems["c"], c_sqrt[NT - 1])
                    inc("v", e.reciprocal(statI[:, NT - 1:NT],
                                          statT[:, NT - 1:NT]))
                    e.wait_ge(sems["v"], cnt["v"])
                    for k in (NT - 3, NT - 1):
                        e.wait_ge(sems["t"], t_trans(k))
                        inc("v", e.tensor_copy(XNT4[:, :, k, :],
                                               ptr[1][:, :, :]))
                        e.wait_ge(sems["v"], cnt["v"])

            if want_xn:
                @b.tensor
                def _(e):
                    for j in range(NT):
                        e.wait_ge(sems["c"], c_apply[j])
                        if j >= 2:
                            w_pcopy(e, j - 2)
                        tr = None
                        for k in range(NT):
                            tr = e.transpose(
                                pt[j % 2][:, k * 128:(k + 1) * 128],
                                stage[j % 2][:, k * 128:(k + 1) * 128],
                                ident_b[:, :])
                        inc("t", tr)

            @b.gpsimd
            def _(e):
                for j in range(NT):
                    if j % 2 == 0:
                        emit_casts(e, 1)
                    if store_x:
                        if addT is not None:
                            e.wait_ge(sems["v"], v_combine[j])
                        inc("gd", e.dma_start(
                            out=x_dst[j * 128:(j + 1) * 128, :],
                            in_=fAb[j % 2][:, :]))
                        e.wait_ge(sems["gd"], cnt["gd"])

    def matmul_phase(wdram_b, rhs_fn, evac_emit, cpi, wname=None):
        with nc.Block() as b, \
             nc.psum_tensor(uname("pA"), [128, 512], F32) as p0, \
             nc.psum_tensor(uname("pB"), [128, 512], F32) as p1, \
             nc.psum_tensor(uname("pC"), [128, 512], F32) as p2, \
             nc.psum_tensor(uname("pD"), [128, 512], F32) as p3, \
             nc.psum_tensor(uname("pE"), [128, 512], F32) as p4, \
             nc.psum_tensor(uname("pF"), [128, 512], F32) as p5, \
             nc.psum_tensor(uname("pG"), [128, 512], F32) as p6, \
             nc.psum_tensor(uname("pH"), [128, 512], F32) as p7:
            ps = [p0, p1, p2, p3, p4, p5, p6, p7]
            bases = {k: cnt[k] for k in cnt}

            wsem = cast_done.get(wname) if wname else None

            @b.sync
            def _(e):
                for m in range(NT):
                    if wsem:
                        e.wait_ge(sems[wsem], 128)
                    if m >= 2:
                        e.wait_ge(sems["t"], bases["t"] + (m - 1))
                    inc("sd", e.dma_start(out=wslab[m % 2][:, :],
                                          in_=wdram_b[m, :, :]), 16)
                    e.wait_ge(sems["sd"], cnt["sd"])

            @b.tensor
            def _(e):
                for m in range(NT):
                    e.wait_ge(sems["sd"], bases["sd"] + 16 * (m + 1))
                    if m >= 2:
                        e.wait_ge(sems["c"], bases["c"] + cpi * (m - 1))
                    mm = None
                    for k in range(NT):
                        for n in range(4):
                            mm = e.matmul(ps[4 * (m % 2) + n][:, :],
                                          wslab[m % 2][:, k * 128:(k + 1) * 128],
                                          rhs_fn(k, n),
                                          start=(k == 0), stop=(k == NT - 1))
                    inc("t", mm)

            evac_emit(b, ps, bases)
        return

    def evac_store(out_hbm, func, bias_name, elu):
        cpi = 5 if elu else 4

        def emit(b, ps, bases):
            tile = [sE, sF]

            @b.scalar
            def _(e):
                for m in range(NT):
                    e.wait_ge(sems["t"], bases["t"] + m + 1)
                    if m >= 2:
                        e.wait_ge(sems["gd"], bases["gd"] + 16 * (m - 1))
                    bias = bias_sb[bias_name][:, m:m + 1] if bias_name else 0.0
                    fn = func
                    if bias_name and fn == AF.Copy:
                        fn = AF.Identity
                    for n in range(4):
                        inc("c", e.activation(
                            tile[m % 2][:, n * 512:(n + 1) * 512],
                            ps[4 * (m % 2) + n][:, :], fn, bias=bias, scale=1.0))
                    if elu:
                        e.wait_ge(sems["v"], bases["v"] + 3 * m + 1)
                        inc("c", e.activation(sD[:, :], sC[:, :], AF.Exp,
                                              bias=0.0, scale=1.0))
                    e.wait_ge(sems["c"], cnt["c"])

            if elu:
                @b.vector
                def _(e):
                    for m in range(NT):
                        e.wait_ge(sems["c"], bases["c"] + cpi * m + 4)
                        inc("v", e.tensor_scalar_min(sC[:, :],
                                                     tile[m % 2][:, :], 0.0))
                        e.wait_ge(sems["v"], cnt["v"])
                        inc("v", e.tensor_scalar_max(sG[:, :],
                                                     tile[m % 2][:, :], 0.0))
                        e.wait_ge(sems["v"], cnt["v"])
                        e.wait_ge(sems["c"], bases["c"] + cpi * (m + 1))
                        inc("v", e.scalar_tensor_tensor(
                            tile[m % 2][:, :], sD[:, :], -1.0, sG[:, :],
                            ALU.add, ALU.add))
                        e.wait_ge(sems["v"], cnt["v"])

            @b.gpsimd
            def _(e):
                for m in range(NT):
                    if m % 4 == 0:
                        emit_casts(e, 1)
                    if elu:
                        e.wait_ge(sems["v"], bases["v"] + 3 * (m + 1))
                    else:
                        e.wait_ge(sems["c"], bases["c"] + cpi * (m + 1))
                    inc("gd", e.dma_start(
                        out=out_hbm[m * 128:(m + 1) * 128, :],
                        in_=tile[m % 2][:, :]))
                    e.wait_ge(sems["gd"], cnt["gd"])
        return emit, cpi

    def ffn(x_src, x_dst, next_want_xn):
        with nc.Block() as b, \
             nc.psum_tensor(uname("ps_l"), [NE, 512], F32) as ps_l:
            t0, v0 = cnt["t"], cnt["v"]

            @b.tensor
            def _(e):
                for n in range(4):
                    if n >= 1:
                        e.wait_ge(sems["v"], v0 + n)
                    mm = None
                    for k in range(NT):
                        mm = e.matmul(ps_l[:, :], gwt[:, k * NE:(k + 1) * NE],
                                      xnt_rhs(k, n),
                                      start=(k == 0), stop=(k == NT - 1))
                    inc("t", mm)

            @b.vector
            def _(e):
                for n in range(4):
                    e.wait_ge(sems["t"], t0 + n + 1)
                    inc("v", e.tensor_scalar(
                        fA[0:NE, n * 512:(n + 1) * 512], ps_l[:, :],
                        50.0, -50.0, ALU.min, ALU.max))
                    e.wait_ge(sems["v"], cnt["v"])

        with nc.Block() as b, \
             nc.psum_tensor(uname("ps_lt"), [128, NE], F32) as ps_lt:
            t0, v0 = cnt["t"], cnt["v"]

            @b.tensor
            def _(e):
                for j in range(NT):
                    if j >= 1:
                        e.wait_ge(sems["v"], v0 + j)
                    inc("t", e.transpose(ps_lt[:, :],
                                         fA[0:NE, j * 128:(j + 1) * 128],
                                         ident_f[0:NE, 0:NE]))

            @b.vector
            def _(e):
                for j in range(NT):
                    e.wait_ge(sems["t"], t0 + j + 1)
                    inc("v", e.tensor_copy(lg_tm[:, j * NE:(j + 1) * NE],
                                           ps_lt[:, :]))
                    e.wait_ge(sems["v"], cnt["v"])

        with nc.Block() as b, \
             nc.psum_tensor(uname("ps_pt"), [NE, 128], BF16) as ps_pt:
            t0, v0, c0 = cnt["t"], cnt["v"], cnt["c"]

            @b.vector
            def _(e):
                def lv(inst):
                    inc("v", inst)
                    e.wait_ge(sems["v"], cnt["v"])
                lv(e.reduce_max(m1c[:, :], g3(lg_tm), axis=AX))
                lv(e.tensor_tensor(g3(eq1), g3(lg_tm), g3b(m1c), ALU.is_equal))
                lv(e.scalar_tensor_tensor(g3(l2m), g3(eq1), -1e30, g3(lg_tm),
                                          ALU.mult, ALU.add))
                lv(e.reduce_max(m2c[:, :], g3(l2m), axis=AX))
                lv(e.tensor_tensor(g3(eq2), g3(l2m), g3b(m2c), ALU.is_equal))
                lv(e.tensor_sub(dc[:, :], m2c[:, :], m1c[:, :]))
                e.wait_ge(sems["c"], c0 + 1)
                lv(e.tensor_scalar_add(rw1c[:, :], dc[:, :], 1.0))
                lv(e.reciprocal(rw1c[:, :], rw1c[:, :]))
                lv(e.tensor_mul(rw2c[:, :], dc[:, :], rw1c[:, :]))
                lv(e.tensor_tensor(g3(p_tm), g3(eq1), g3b(rw1c), ALU.mult))
                lv(e.tensor_tensor(g3(eq2), g3(eq2), g3b(rw2c), ALU.mult))
                lv(e.tensor_add(p_tm[:, :], p_tm[:, :], eq2[:, :]))
                lv(e.tensor_copy(pb_tm[:, :], p_tm[:, :]))

            @b.scalar
            def _(e):
                e.wait_ge(sems["v"], v0 + 6)
                inc("c", e.activation(dc[:, :], dc[:, :], AF.Exp,
                                      bias=0.0, scale=1.0))
                e.wait_ge(sems["c"], cnt["c"])

            @b.tensor
            def _(e):
                e.wait_ge(sems["v"], v0 + 13)
                for j in range(NT):
                    if j >= 1:
                        e.wait_ge(sems["v"], v0 + 13 + j)
                    inc("t", e.transpose(ps_pt[:, :],
                                         pb_tm[:, j * NE:(j + 1) * NE],
                                         ident_b[:, :]))

            @b.vector
            def _(e):
                for j in range(NT):
                    e.wait_ge(sems["t"], t0 + j + 1)
                    inc("v", e.tensor_copy(pT[:, j * 128:(j + 1) * 128],
                                           ps_pt[:, :]))
                    e.wait_ge(sems["v"], cnt["v"])

        with nc.Block() as b, \
             nc.psum_tensor(uname("q0"), [128, 2048], F32) as q0_, \
             nc.psum_tensor(uname("q1"), [128, 2048], F32) as q1_:
            qs = [q0_, q1_]
            xg = [sA, sB, sX, sY]
            xsem = ["x0", "x1", "x2", "x3"]
            vt = [sC, sD]
            acc = [sE, sF]
            t0, v0, c0, s0, sd0 = (cnt["t"], cnt["v"], cnt["c"], cnt["gd"],
                                   cnt["sd"])
            x0c = {nm: cnt[nm] for nm in xsem}
            sched = []
            spt_of_pt = {}
            for pt in range(NT):
                if tiles_j[pt] == 0:
                    continue
                spt = len(spt_of_pt)
                spt_of_pt[pt] = spt
                for j in range(tiles_j[pt]):
                    sched.append((spt, pt, j))
            assert len(sched) == nst
            last_st = {}
            for i, (spt, pt, j) in enumerate(sched):
                last_st[spt] = i
            inv_last = {v: k for k, v in last_st.items()}
            pt_of_spt = {v: k for k, v in spt_of_pt.items()}
            ngrp_l = (nst + 15) // 16

            @b.sync
            def _(e):
                for g in range(ngrp_l):
                    if g >= 2:
                        e.wait_ge(sems["t"], t0 + 16 * (g - 1))
                    inc("sd", e.dma_start(
                        out=wchunk[g % 2][:, :],
                        in_=weighted_hbm[:, g * 2048:(g + 1) * 2048]), 16)
                    e.wait_ge(sems["sd"], cnt["sd"])

            @b.tensor
            def _(e):
                for st in range(nst):
                    g = st // 16
                    e.wait_ge(sems["sd"], sd0 + 16 * (g + 1))
                    if st >= 2:
                        e.wait_ge(sems["c"], c0 + (st - 1))
                    mm = None
                    for n in range(4):
                        mm = e.matmul(qs[st % 2][:, n * 512:(n + 1) * 512],
                                      wchunk[g % 2][:, (st % 16) * 128:
                                                    (st % 16 + 1) * 128],
                                      pT[:, n * 512:(n + 1) * 512],
                                      start=True, stop=True)
                    inc("t", mm)

            @b.scalar
            def _(e):
                for st, (spt, pt, j) in enumerate(sched):
                    e.wait_ge(sems["t"], t0 + st + 1)
                    if st >= 2:
                        e.wait_ge(sems["v"], v0 + 2 * (st - 1))
                    inc("c", e.activation(
                        vt[st % 2][:, :], qs[st % 2][:, :], AF.Copy,
                        bias=0.0, scale=1.0))
                    e.wait_ge(sems["c"], cnt["c"])

            @b.gpsimd
            def _(e):
                for st in range(nst):
                    if st % 12 == 0:
                        emit_casts(e, 1)
                    if st >= 4:
                        e.wait_ge(sems["v"], v0 + 2 * (st - 3))
                    inc(xsem[st % 4], e.indirect_dma_start(
                        xg[st % 4][:, :], None, xnT_hbm[:, :],
                        IndirectOffsetOnAxis(colidx[:, st:st + 1], 0)), 16)
                    if st in inv_last:
                        spt_l = inv_last[st]
                        e.wait_ge(sems["v"], v0 + 2 * (st + 1))
                        inc("gd", e.dma_start(
                            out=contrib_hbm[pt_of_spt[spt_l] * 128:
                                            (pt_of_spt[spt_l] + 1) * 128, :],
                            in_=acc[spt_l % 2][:, :]))
                        e.wait_ge(sems["gd"], cnt["gd"])

            @b.vector
            def _(e):
                for st, (spt, pt, j) in enumerate(sched):
                    e.wait_ge(sems[xsem[st % 4]],
                              x0c[xsem[st % 4]] + 16 * (st // 4 + 1))
                    e.wait_ge(sems["c"], c0 + (st + 1))
                    if j == 0:
                        if spt >= 2:
                            e.wait_ge(sems["gd"], s0 + 16 * (spt - 1))
                        inc("v", e.tensor_mul(acc[spt % 2][:, :],
                                              xg[st % 4][:, :],
                                              vt[st % 2][:, :]))
                        inc("v", e.engine_nop())
                        e.wait_ge(sems["v"], cnt["v"])
                    else:
                        inc("v", e.tensor_mul(wslab[st % 2][:, :],
                                              xg[st % 4][:, :],
                                              vt[st % 2][:, :]))
                        e.wait_ge(sems["v"], cnt["v"])
                        inc("v", e.tensor_add(acc[spt % 2][:, :],
                                              acc[spt % 2][:, :],
                                              wslab[st % 2][:, :]))
                        e.wait_ge(sems["v"], cnt["v"])


        def evac_w(b, ps, bases):
            tile = [sE, sF]
            cbuf = [sC, sD]

            @b.scalar
            def _(e):
                for m in range(NT):
                    e.wait_ge(sems["t"], bases["t"] + m + 1)
                    for n in range(4):
                        inc("c", e.activation(
                            tile[m % 2][:, n * 512:(n + 1) * 512],
                            ps[4 * (m % 2) + n][:, :], AF.Copy, bias=0.0, scale=1.0))
                    e.wait_ge(sems["c"], cnt["c"])
                    e.wait_ge(sems["v"], bases["v"] + (m + 1))
                    inc("se", e.dma_start(
                        out=outT_hbm[m * 128:(m + 1) * 128, :],
                        in_=tile[m % 2][:, :]))
                    e.wait_ge(sems["se"], cnt["se"])

            @b.gpsimd
            def _(e):
                for m in range(NT):
                    if m % 4 == 0:
                        emit_casts(e, 1)
                    if m >= 2:
                        e.wait_ge(sems["v"], bases["v"] + (m - 1))
                    inc("gd", e.indirect_dma_start(
                        cbuf[m % 2][:, :], None, contrib_hbm[:, :],
                        IndirectOffsetOnAxis(sigma_sb[:, m:m + 1], 0)))
                    e.wait_ge(sems["gd"], cnt["gd"])

            @b.vector
            def _(e):
                for m in range(NT):
                    e.wait_ge(sems["gd"], bases["gd"] + 16 * (m + 1))
                    e.wait_ge(sems["c"], bases["c"] + 4 * (m + 1))
                    inc("v", e.tensor_add(tile[m % 2][:, :], tile[m % 2][:, :],
                                          cbuf[m % 2][:, :]))
                    e.wait_ge(sems["v"], cnt["v"])


        matmul_phase(wb["ww"], xnt_rhs, evac_w, cpi=4, wname="ww")
        residual_ln_pass(x_src, outT_hbm, 0.5, x_dst,
                         want_xn=next_want_xn, want_hbm=False)

    def attention(x_src, x_dst):
        em, cpi = evac_store(qT_hbm, AF.Copy, "bqk", elu=True)
        matmul_phase(wb["wq"], xnt_rhs, em, cpi, wname="wq")
        em, cpi = evac_store(vT_hbm, AF.Silu, "bv", elu=False)
        matmul_phase(wb["wv"], xnt_rhs, em, cpi, wname="wv")

        with nc.Block() as b, \
             nc.psum_tensor(uname("pkv"), [128, HEADS * 128], F32) as pkv, \
             nc.psum_tensor(uname("u0"), [128, 512], F32) as u0_, \
             nc.psum_tensor(uname("u1"), [128, 512], F32) as u1_, \
             nc.psum_tensor(uname("u2"), [128, 512], F32) as u2_, \
             nc.psum_tensor(uname("u3"), [128, 512], F32) as u3_:
            us_ = [u0_, u1_, u2_, u3_]
            utile = [sE, sF]
            s0, t0, v0, c0, g0 = (cnt["sd"], cnt["t"], cnt["v"], cnt["c"],
                                  cnt["gd"])
            wsem_u = cast_done.get("wu")
            kvinc = {}
            utinc = {}
            pos = 0
            units_ku = []
            for m in range(NT):
                if m < 8:
                    for jj in (2 * m, 2 * m + 1):
                        pos += 1
                        kvinc[jj] = pos
                        units_ku.append(("kv", jj))
                pos += 1
                utinc[m] = pos
                units_ku.append(("uT", m))
            sd_q = {}
            sd_s = {}
            run = s0
            loads = []
            for u in units_ku:
                if u[0] == "kv":
                    loads.append(("q", u[1]))
                    loads.append(("v", u[1]))
                else:
                    loads.append(("s", u[1]))
            for kind, idx in loads:
                run += 16
                if kind == "v":
                    sd_q[idx] = run
                elif kind == "s":
                    sd_s[idx] = run

            @b.sync
            def _(e):
                for kind, idx in loads:
                    if kind == "q":
                        if idx >= 2:
                            e.wait_ge(sems["t"], t0 + kvinc[idx - 2])
                        inc("sd", e.dma_start(
                            out=(sA if idx % 2 == 0 else sC)[:, :],
                            in_=qT_hbm[:, idx * 128:(idx + 1) * 128],
                            transpose=True), 16)
                    elif kind == "v":
                        inc("sd", e.dma_start(
                            out=(sB if idx % 2 == 0 else sD)[:, :],
                            in_=vT_hbm[:, idx * 128:(idx + 1) * 128],
                            transpose=True), 16)
                    else:
                        if wsem_u:
                            e.wait_ge(sems[wsem_u], 128)
                        if idx >= 2:
                            e.wait_ge(sems["t"], t0 + utinc[idx - 2])
                        inc("sd", e.dma_start(out=wslab[idx % 2][:, :],
                                              in_=wb["wu"][idx, :, :]), 16)
                    e.wait_ge(sems["sd"], cnt["sd"])

            @b.tensor
            def _(e):
                for u in units_ku:
                    if u[0] == "kv":
                        j = u[1]
                        e.wait_ge(sems["sd"], sd_q[j])
                        if j >= 1:
                            e.wait_ge(sems["v"], v0 + j)
                        qa = sA if j % 2 == 0 else sC
                        va = sB if j % 2 == 0 else sD
                        mm = None
                        for h in range(HEADS):
                            mm = e.matmul(pkv[:, h * 128:(h + 1) * 128],
                                          qa[:, h * 128:(h + 1) * 128],
                                          va[:, h * 128:(h + 1) * 128],
                                          start=True, stop=True)
                        inc("t", mm)
                    else:
                        m = u[1]
                        e.wait_ge(sems["sd"], sd_s[m])
                        if m >= 1:
                            e.wait_ge(sems["c"], c0 + 4 * m)
                        mm = None
                        for k in range(NT):
                            for n in range(4):
                                mm = e.matmul(
                                    us_[n][:, :],
                                    wslab[m % 2][:, k * 128:(k + 1) * 128],
                                    xnt_rhs(k, n),
                                    start=(k == 0), stop=(k == NT - 1))
                        inc("t", mm)

            @b.vector
            def _(e):
                for j in range(NT):
                    e.wait_ge(sems["t"], t0 + kvinc[j])
                    if j == 0:
                        inc("v", e.tensor_copy(kv_sb[:, :], pkv[:, :]))
                    else:
                        inc("v", e.tensor_add(kv_sb[:, :], kv_sb[:, :],
                                              pkv[:, :]))
                    e.wait_ge(sems["v"], cnt["v"])

            @b.scalar
            def _(e):
                for m in range(NT):
                    e.wait_ge(sems["t"], t0 + utinc[m])
                    if m >= 2:
                        e.wait_ge(sems["gd"], g0 + 16 * (m - 1) +
                                  (16 if m - 2 >= 8 else 0))
                    for n in range(4):
                        inc("c", e.activation(
                            utile[m % 2][:, n * 512:(n + 1) * 512],
                            us_[n][:, :], AF.Silu,
                            bias=bias_sb["bu"][:, m:m + 1], scale=1.0))
                    e.wait_ge(sems["c"], cnt["c"])

            @b.gpsimd
            def _(e):
                for m in range(NT):
                    if m % 4 == 0:
                        emit_casts(e, 1)
                    e.wait_ge(sems["c"], c0 + 4 * (m + 1))
                    inc("gd", e.dma_start(
                        out=uT_hbm[m * 128:(m + 1) * 128, :],
                        in_=utile[m % 2][:, :]))
                    e.wait_ge(sems["gd"], cnt["gd"])
                    if m == 8:
                        e.wait_ge(sems["v"], v0 + NT)
                        inc("gd", e.dma_start(out=cc_in[:, :], in_=kv_sb[:, :]))
                        e.wait_ge(sems["gd"], cnt["gd"])
                        inc("cc", e.collective_compute(
                            "AllReduce", ALU.add,
                            replica_groups=[[0, 1, 2, 3], [4, 5, 6, 7]],
                            ins=[cc_in.ap().opt()],
                            outs=[cc_out.ap().opt()]))
        with nc.Block() as b, \
             nc.psum_tensor(uname("o0"), [128, 512], F32) as o0_, \
             nc.psum_tensor(uname("o1"), [128, 512], F32) as o1_, \
             nc.psum_tensor(uname("o2"), [128, 512], F32) as o2_, \
             nc.psum_tensor(uname("o3"), [128, 512], F32) as o3_:
            os_ = [o0_, o1_, o2_, o3_]
            s0, t0, c0, v0, g0 = (cnt["sd"], cnt["t"], cnt["c"], cnt["v"],
                                  cnt["gd"])

            @b.gpsimd
            def _(e):
                e.wait_ge(sems["cc"], cnt["cc"])
                inc("gd", e.dma_start(out=kv_sb[:, :], in_=cc_out[:, :]))
                e.wait_ge(sems["gd"], cnt["gd"])

            @b.vector
            def _(e):
                e.wait_ge(sems["gd"], g0 + 16)
                inc("v", e.tensor_copy(kvb[:, :], kv_sb[:, :]))
                e.wait_ge(sems["v"], cnt["v"])

            @b.sync
            def _(e):
                for h in range(HEADS):
                    if h >= 2:
                        e.wait_ge(sems["t"], t0 + (h - 1))
                    inc("sd", e.dma_start(
                        out=wslab[h % 2][:, :],
                        in_=qT_hbm[h * 128:(h + 1) * 128, :]), 16)
                    e.wait_ge(sems["sd"], cnt["sd"])

            @b.tensor
            def _(e):
                e.wait_ge(sems["v"], v0 + 1)
                for h in range(HEADS):
                    e.wait_ge(sems["sd"], s0 + 16 * (h + 1))
                    if h >= 1:
                        e.wait_ge(sems["c"], c0 + 4 * h)
                    mm = None
                    for n in range(4):
                        mm = e.matmul(os_[n][:, :],
                                      kvb[:, h * 128:(h + 1) * 128],
                                      wslab[h % 2][:, n * 512:(n + 1) * 512],
                                      start=True, stop=True)
                    inc("t", mm)

            @b.scalar
            def _(e):
                for h in range(HEADS):
                    e.wait_ge(sems["t"], t0 + h + 1)
                    for n in range(4):
                        inc("c", e.activation(
                            XNT[:, h * TC + n * 512:h * TC + (n + 1) * 512],
                            os_[n][:, :], AF.Copy, bias=0.0, scale=1.0))
                    e.wait_ge(sems["c"], cnt["c"])

        with nc.Block() as b, \
             nc.psum_tensor(uname("ps_r"), [1, TC], F32) as ps_r, \
             nc.psum_tensor(uname("ps_r2"), [1, TC], F32) as ps_r2:
            t0, c0, v0 = cnt["t"], cnt["c"], cnt["v"]

            @b.scalar
            def _(e):
                for h in range(HEADS):
                    if h >= 1:
                        e.wait_ge(sems["t"], t0 + h)
                    inc("c", e.activation(sG[:, :], XNT[:, h * TC:(h + 1) * TC],
                                          AF.Square, bias=0.0, scale=1.0))
                    e.wait_ge(sems["c"], cnt["c"])
                e.wait_ge(sems["v"], v0 + 4)
                inc("c", e.activation(row_t[0:1, 0:TC], fB[0:1, 0:TC], AF.Sqrt,
                                      bias=epsv[0:1, 0:1], scale=1.0))
                e.wait_ge(sems["c"], cnt["c"])

            @b.tensor
            def _(e):
                for h in range(HEADS):
                    e.wait_ge(sems["c"], c0 + h + 1)
                    mm = None
                    for n in range(4):
                        mm = e.matmul(ps_r[:, n * 512:(n + 1) * 512],
                                      ones_k[:, :],
                                      XNT[:, h * TC + n * 512:
                                          h * TC + (n + 1) * 512],
                                      start=(h == 0), stop=(h == HEADS - 1))
                    for n in range(4):
                        mm = e.matmul(ps_r2[:, n * 512:(n + 1) * 512],
                                      ones_k[:, :],
                                      sG[:, n * 512:(n + 1) * 512],
                                      start=(h == 0), stop=(h == HEADS - 1))
                    inc("t", mm)

            @b.vector
            def _(e):
                e.wait_ge(sems["t"], t0 + HEADS)
                inc("v", e.tensor_scalar_mul(fA[0:1, 0:TC], ps_r[:, :], 1.0 / H))
                inc("v", e.tensor_scalar_mul(fB[0:1, 0:TC], ps_r2[:, :], 1.0 / H))
                e.wait_ge(sems["v"], cnt["v"])
                inc("v", e.tensor_mul(row_t[0:1, 0:TC], fA[0:1, 0:TC],
                                      fA[0:1, 0:TC]))
                e.wait_ge(sems["v"], cnt["v"])
                inc("v", e.tensor_sub(fB[0:1, 0:TC], fB[0:1, 0:TC],
                                      row_t[0:1, 0:TC]))
                e.wait_ge(sems["v"], cnt["v"])
                e.wait_ge(sems["c"], c0 + HEADS + 1)
                inc("v", e.reciprocal(fB[0:1, 0:TC], row_t[0:1, 0:TC]))
                e.wait_ge(sems["v"], cnt["v"])
                inc("v", e.tensor_mul(row_t[0:1, 0:TC], fA[0:1, 0:TC],
                                      fB[0:1, 0:TC]))
                e.wait_ge(sems["v"], cnt["v"])
                inc("v", e.tensor_scalar_mul(row_t[0:1, 0:TC],
                                             row_t[0:1, 0:TC], -1.0))
                e.wait_ge(sems["v"], cnt["v"])
                inc("v", e.tensor_copy(rowba[:, :], fB[0:1, 0:TC]))
                inc("v", e.tensor_copy(rowbb[:, :], row_t[0:1, 0:TC]))
                e.wait_ge(sems["v"], cnt["v"])

        with nc.Block() as b, \
             nc.psum_tensor(uname("pb0"), [128, 512], F32) as pb0, \
             nc.psum_tensor(uname("pb1"), [128, 512], F32) as pb1:
            t0, v0, c0, s0 = cnt["t"], cnt["v"], cnt["c"], cnt["sd"]

            @b.tensor
            def _(e):
                for n in range(4):
                    if n >= 1:
                        e.wait_ge(sems["v"], v0 + 2 * n)
                    e.matmul(pb0[:, :], ones_1[:, :],
                             rowba[:, n * 512:(n + 1) * 512],
                             start=True, stop=True)
                    inc("t", e.matmul(pb1[:, :], ones_1[:, :],
                                      rowbb[:, n * 512:(n + 1) * 512],
                                      start=True, stop=True))

            @b.vector
            def _(e):
                for n in range(4):
                    e.wait_ge(sems["t"], t0 + n + 1)
                    inc("v", e.tensor_copy(sA[:, n * 512:(n + 1) * 512],
                                           pb0[:, :]))
                    e.wait_ge(sems["v"], cnt["v"])
                    inc("v", e.tensor_copy(sB[:, n * 512:(n + 1) * 512],
                                           pb1[:, :]))
                    e.wait_ge(sems["v"], cnt["v"])

            vb = v0 + 8

            @b.sync
            def _(e):
                for h in range(HEADS):
                    if h >= 2:
                        e.wait_ge(sems["v"], vb + 3 * (h - 1))
                    inc("sd", e.dma_start(
                        out=(sC if h % 2 == 0 else sD)[:, :],
                        in_=uT_hbm[h * 128:(h + 1) * 128, :]), 16)
                    e.wait_ge(sems["sd"], cnt["sd"])

            @b.vector
            def _(e):
                for h in range(HEADS):
                    e.wait_ge(sems["sd"], s0 + 16 * (h + 1))
                    inc("v", e.tensor_mul(sE[:, :], XNT[:, h * TC:(h + 1) * TC],
                                          sA[:, :]))
                    e.wait_ge(sems["v"], cnt["v"])
                    inc("v", e.tensor_add(sE[:, :], sE[:, :], sB[:, :]))
                    e.wait_ge(sems["v"], cnt["v"])
                    e.wait_ge(sems["c"], c0 + h + 1)
                    inc("v", e.tensor_mul(XNT[:, h * TC:(h + 1) * TC],
                                          sG[:, :],
                                          (sC if h % 2 == 0 else sD)[:, :]))
                    e.wait_ge(sems["v"], cnt["v"])

            @b.scalar
            def _(e):
                for h in range(HEADS):
                    e.wait_ge(sems["v"], vb + 3 * h + 2)
                    inc("c", e.activation(sG[:, :], sE[:, :], AF.Identity,
                                          bias=bias_sb["lnb"][:, h:h + 1],
                                          scale=bias_sb["lng"][:, h:h + 1]))
                    e.wait_ge(sems["c"], cnt["c"])

        em, cpi = evac_store(outT_hbm, AF.Copy, "bo", elu=False)
        matmul_phase(wb["wo"], xnt_rhs, em, cpi, wname="wo")
        residual_ln_pass(x_src, outT_hbm, 1.0, x_dst,
                         want_xn=True, want_hbm=True)

    residual_ln_pass(x_in, None, 0.0, x_a, want_xn=True, want_hbm=True)
    ffn(x_in, x_a, next_want_xn=True)
    attention(x_a, x_b)
    ffn(x_b, y_out, next_want_xn=False)

    with nc.Block() as b:
        @b.sync
        def _(e):
            e.wait_ge(sems["gd"], cnt["gd"])
            e.wait_ge(sems["sd"], cnt["sd"])
            e.wait_ge(sems["se"], cnt["se"])

    stack.close()
    return nc


def kernel(**inputs):
    x = np.asarray(inputs["L0_attn_x"], dtype=np.float32).reshape(-1, H)
    mask_idx = np.asarray(inputs["L0_ffn_mask_idx"]).astype(np.int64)
    plan = _plan_slots(mask_idx)
    nst = plan["nst"]
    nslot = nst * 128

    wq = _pack_lhsT(np.asarray(inputs["L0_attn_Wqk"], dtype=np.float32))
    wv = _pack_lhsT(np.asarray(inputs["L0_attn_Wv"], dtype=np.float32))
    wu = _pack_lhsT(np.asarray(inputs["L0_attn_Wu"], dtype=np.float32))
    wo = _pack_lhsT(np.asarray(inputs["L0_attn_Wo"], dtype=np.float32))
    ww = _pack_lhsT(np.ascontiguousarray(
        np.asarray(inputs["L0_ffn_W"], dtype=np.float32).T))

    gw = np.asarray(inputs["L0_ffn_gate_w"], dtype=np.float32)
    gwt = np.ascontiguousarray(
        gw.T.reshape(NT, 128, NE).transpose(1, 0, 2).reshape(128, NT * NE))

    def tile_cols(v):
        return np.ascontiguousarray(v.reshape(NT, 128).T).astype(np.float32)

    bqk = tile_cols(np.asarray(inputs["L0_attn_bqk"], dtype=np.float32))
    bv = tile_cols(np.asarray(inputs["L0_attn_bv"], dtype=np.float32))
    bu = tile_cols(np.asarray(inputs["L0_attn_bu"], dtype=np.float32))
    bo = tile_cols(np.asarray(inputs["L0_attn_bo"], dtype=np.float32))
    lng = tile_cols(np.asarray(inputs["L0_attn_ln_g"], dtype=np.float32))
    lnb = tile_cols(np.asarray(inputs["L0_attn_ln_b"], dtype=np.float32))

    atoms = np.asarray(inputs["L0_ffn_atoms"], dtype=np.float32)
    imp = np.asarray(inputs["L0_ffn_importance"], dtype=np.float32)
    slot_s = plan["slot_s"].reshape(-1)
    valid = slot_s >= 0
    atoms_s = np.zeros((NA, nslot), dtype=np.float32)
    atoms_s[:, valid] = atoms[:, slot_s[valid]]
    imp_s = np.zeros((NE, nslot), dtype=np.float32)
    imp_s[:, valid] = imp[:, slot_s[valid]]
    colidx = np.ascontiguousarray(plan["slot_c"].astype(np.int32).T)
    sigma = np.ascontiguousarray(
        plan["sigma"].astype(np.int32).reshape(NT, 128).T)

    eaw = np.asarray(inputs["L0_ffn_expert_atom_weights"], dtype=np.float32)
    ident_b = np.eye(128, dtype=np.float32).astype(ml_dtypes.bfloat16)
    ident_f = np.eye(128, dtype=np.float32)

    nc = build_program(plan["tiles_j"], nst)

    shared = {"wq": wq, "wv": wv, "wu": wu, "wo": wo, "ww": ww, "gwt": gwt,
              "bqk": bqk, "bv": bv, "bu": bu, "bo": bo, "lng": lng, "lnb": lnb,
              "eaw": eaw, "atoms_s": atoms_s, "imp_s": imp_s, "colidx": colidx,
              "sigma": sigma,
              "ident_b": ident_b, "ident_f": ident_f}
    in_maps = []
    for c in range(8):
        m = dict(shared)
        m["x"] = np.ascontiguousarray(x[c * TC:(c + 1) * TC])
        in_maps.append(m)

    res = run_bass_kernel_spmd(nc, in_maps, core_ids=list(range(8)))
    out = np.concatenate([res.results[c]["y"] for c in range(8)], axis=0)
    return out.reshape(2, 8192, H).astype(np.float32)


# version 24 (final)
# speedup vs baseline: 1.0410x; 1.0044x over previous
import numpy as np
import ml_dtypes

import concourse.bass as bass
import concourse.mybir as mybir
from concourse.bass import IndirectOffsetOnAxis
from concourse.bass_utils import run_bass_kernel_spmd

F32 = mybir.dt.float32
BF16 = mybir.dt.bfloat16
I32 = mybir.dt.int32
AF = mybir.ActivationFunctionType
ALU = mybir.AluOpType
AX = mybir.AxisListType.X

H = 2048
NT = 16
TC = 2048
NS = 16384
NE = 8
NA = 64
EPS = 1e-5
HEADS = 16


def _pack_lhsT(w):
    return np.ascontiguousarray(
        w.reshape(NT, 128, NT, 128).transpose(2, 1, 0, 3).reshape(NT, 128, H)
    ).astype(np.float32)


def _plan_slots(mask_idx):
    rows = (mask_idx // H).astype(np.int64)
    cols = (mask_idx % H).astype(np.int64)
    order = np.argsort(rows, kind="stable")
    counts = np.bincount(rows, minlength=H)
    row_start = np.zeros(H, dtype=np.int64)
    row_start[1:] = np.cumsum(counts)[:-1]
    rowperm = np.argsort(-counts, kind="stable")
    tiles_j, slot_s, slot_c = [], [], []
    sigma = np.full(H, H, dtype=np.int64)
    for pt in range(NT):
        rs = rowperm[pt * 128:(pt + 1) * 128]
        jmax = int(counts[rs].max())
        tiles_j.append(jmax)
        if jmax > 0:
            sigma[rs] = pt * 128 + np.arange(128)
        for j in range(jmax):
            s_ent = np.full(128, -1, dtype=np.int64)
            c_ent = np.zeros(128, dtype=np.int64)
            sel = counts[rs] > j
            s_ent[sel] = order[row_start[rs[sel]] + j]
            c_ent[sel] = cols[s_ent[sel]]
            slot_s.append(s_ent)
            slot_c.append(c_ent)
    return {"tiles_j": tiles_j,
            "slot_s": np.stack(slot_s),
            "slot_c": np.stack(slot_c),
            "sigma": sigma,
            "nst": len(slot_s)}


def build_program(tiles_j, nst):
    import contextlib
    nslot = nst * 128
    nc = bass.Bass("TRN2", target_bir_lowering=False)
    stack = contextlib.ExitStack()

    x_in = nc.dram_tensor("x", [TC, H], F32, kind="ExternalInput")
    y_out = nc.dram_tensor("y", [TC, H], F32, kind="ExternalOutput")
    w_in = {nm: nc.dram_tensor(nm, [NT, 128, H], F32, kind="ExternalInput")
            for nm in ("wq", "wv", "wu", "wo", "ww")}
    gwt_d = nc.dram_tensor("gwt", [128, NT * NE], F32, kind="ExternalInput")
    b_in = {nm: nc.dram_tensor(nm, [128, NT], F32, kind="ExternalInput")
            for nm in ("bqk", "bv", "bu", "bo", "lng", "lnb")}
    eaw_d = nc.dram_tensor("eaw", [NE, NA], F32, kind="ExternalInput")
    atoms_d = nc.dram_tensor("atoms_s", [NA, nslot], F32, kind="ExternalInput")
    imp_d = nc.dram_tensor("imp_s", [NE, nslot], F32, kind="ExternalInput")
    colidx_d = nc.dram_tensor("colidx", [128, nst], I32, kind="ExternalInput")
    sigma_d = nc.dram_tensor("sigma", [128, NT], I32, kind="ExternalInput")
    identb_d = nc.dram_tensor("ident_b", [128, 128], BF16, kind="ExternalInput")
    identf_d = nc.dram_tensor("ident_f", [128, 128], F32, kind="ExternalInput")

    wb = {nm: nc.dram_tensor(nm + "_b", [NT, 128, H], BF16) for nm in w_in}
    x_a = nc.dram_tensor("x_a", [TC, H], F32)
    x_b = nc.dram_tensor("x_b", [TC, H], F32)
    xn_hbm = nc.dram_tensor("xn_hbm", [TC, H], BF16)
    xnT_hbm = nc.dram_tensor("xnT_hbm", [H, TC], BF16)
    qT_hbm = nc.dram_tensor("qT_hbm", [H, TC], BF16)
    vT_hbm = nc.dram_tensor("vT_hbm", [H, TC], BF16)
    uT_hbm = nc.dram_tensor("uT_hbm", [H, TC], BF16)
    outT_hbm = nc.dram_tensor("outT_hbm", [H, TC], BF16)
    contrib_hbm = nc.dram_tensor("contrib_hbm", [H + 128, TC], BF16)
    cc_in = nc.dram_tensor("cc_in", [128, HEADS * 128], F32)
    cc_out = nc.dram_tensor("cc_out", [128, HEADS * 128], F32)
    ngrp = (nst + 15) // 16
    nslot_pad = ngrp * 2048
    weighted_hbm = nc.dram_tensor("weighted_hbm", [NE, nslot_pad], BF16)

    sems = {}
    cnt = {}
    for nm in ("sd", "se", "gd", "cc", "t", "v", "c", "g",
               "x0", "x1", "x2", "x3", "wc", "e0", "e1",
               "w1", "w2", "w3", "w4", "w5", "tq", "sh", "pa"):
        sems[nm] = stack.enter_context(nc.semaphore(nm))
        cnt[nm] = 0

    def inc(nm, inst, k=1):
        if nm in ("sd", "se", "gd", "x0", "x1", "x2", "x3", "wc",
                  "e0", "e1", "w1", "w2", "w3", "w4", "w5", "tq", "sh"):
            k = 16
        inst.then_inc(sems[nm], k)
        cnt[nm] += k
        return cnt[nm]

    _uid = [0]

    def uname(p):
        _uid[0] += 1
        return f"{p}_{_uid[0]}"

    def sb(name, shape, dt):
        return stack.enter_context(nc.sbuf_tensor(name, shape, dt))

    XNT = sb("XNT", [128, NT * TC], BF16)
    wslab = [sb(f"wsl{i}", [128, H], BF16) for i in range(2)]
    sA = sb("sA", [128, H], BF16)
    sB = sb("sB", [128, H], BF16)
    sC = sb("sC", [128, H], BF16)
    sD = sb("sD", [128, H], BF16)
    sE = sb("sE", [128, H], BF16)
    sF = sb("sF", [128, H], BF16)
    sG = sb("sG", [128, H], BF16)
    fA = sb("fA", [128, H], F32)
    fB = sb("fB", [128, H], F32)
    kv_sb = sb("kv_sb", [128, HEADS * 128], F32)
    kvb = sb("kvb", [128, HEADS * 128], BF16)
    ident_b = sb("identb", [128, 128], BF16)
    ident_f = sb("identf", [128, 128], F32)
    gwt = sb("gwt_sb", [128, NT * NE], BF16)
    bias_sb = {nm: sb(nm + "_sb", [128, NT], F32) for nm in b_in}
    colidx = sb("colidx_sb", [128, nst], I32)
    sigma_sb = sb("sigma_sb", [128, NT], I32)
    sX = sb("sX", [128, H], BF16)
    sY = sb("sY", [128, H], BF16)
    wchunk = [sb(f"wch{i}", [NE, 2048], BF16) for i in range(2)]
    ones_k = sb("ones_k", [128, 1], BF16)
    epsv = sb("epsv", [128, 1], F32)
    zerov = sb("zerov", [128, 1], F32)
    ones_1 = sb("ones_1", [1, 128], BF16)
    statS = sb("statS", [128, NT], F32)
    statQ = sb("statQ", [128, NT], F32)
    statM = sb("statM", [128, NT], F32)
    statV = sb("statV", [128, NT], F32)
    statT = sb("statT", [128, NT], F32)
    statI = sb("statI", [128, NT], F32)
    lg_tm = sb("lg_tm", [128, NT * NE], F32)
    eq1 = sb("eq1", [128, NT * NE], F32)
    eq2 = sb("eq2", [128, NT * NE], F32)
    l2m = sb("l2m", [128, NT * NE], F32)
    p_tm = sb("p_tm", [128, NT * NE], F32)
    pb_tm = sb("pb_tm", [128, NT * NE], BF16)
    m1c = sb("m1c", [128, NT], F32)
    m2c = sb("m2c", [128, NT], F32)
    dc = sb("dc", [128, NT], F32)
    rw1c = sb("rw1c", [128, NT], F32)
    rw2c = sb("rw2c", [128, NT], F32)
    pT = sb("pT", [NE, TC], BF16)
    row_t = sb("row_t", [1, TC], F32)
    rowba = sb("rowba", [1, TC], BF16)
    rowbb = sb("rowbb", [1, TC], BF16)

    def g3(t):
        return bass.AP(t, 0, [[NT * NE, 128], [NE, NT], [1, NE]])

    def g3b(t):
        return bass.AP(t, 0, [[NT, 128], [1, NT], [0, NE]])

    def xnt_rhs(k, n):
        return XNT[:, k * TC + n * 512: k * TC + (n + 1) * 512]

    cast_pending = []
    for nm, sm in (("ww", "w1"), ("wq", "w2"), ("wv", "w3"),
                   ("wu", "w4"), ("wo", "w5")):
        for piece in range(8):
            cast_pending.append((nm, sm, piece))
    cast_done = {}

    def emit_casts(e, k):
        for _ in range(k):
            if not cast_pending:
                return
            nm, sm, piece = cast_pending.pop(0)
            inc(sm, e.dma_start(
                out=wb[nm][piece * 2:piece * 2 + 2, :, :],
                in_=w_in[nm][piece * 2:piece * 2 + 2, :, :]))
            if piece == 7:
                cast_done[nm] = sm

    with nc.Block() as b:
        @b.gpsimd
        def _(e):
            inc("wc", e.dma_start(out=gwt[:, :], in_=gwt_d[:, :]))
            inc("g", e.memset(ones_k[:, :], 1.0))
            inc("g", e.memset(ones_1[:, :], 1.0))
            inc("g", e.memset(epsv[:, :], EPS))
            inc("g", e.memset(zerov[:, :], 0.0))
            inc("g", e.memset(sG[:, :], 0.0))
            e.wait_ge(sems["g"], cnt["g"])
            inc("wc", e.dma_start(out=contrib_hbm[H:H + 128, :], in_=sG[:, :]))
            e.wait_ge(sems["wc"], cnt["wc"])

        @b.sync
        def _(e):
            inc("sd", e.dma_start(out=ident_b[:, :], in_=identb_d[:, :]), 16)
            inc("sd", e.dma_start(out=ident_f[:, :], in_=identf_d[:, :]), 16)
            for nm in b_in:
                inc("sd", e.dma_start(out=bias_sb[nm][:, :], in_=b_in[nm][:, :]), 16)
            inc("sd", e.dma_start(out=colidx[:, :], in_=colidx_d[:, :]), 16)
            inc("sd", e.dma_start(out=sigma_sb[:, :], in_=sigma_d[:, :]), 16)
            e.wait_ge(sems["sd"], cnt["sd"])

    eawT = sb("eawT", [NA, NE], BF16)
    with nc.Block() as b, \
         nc.sbuf_tensor("eaw_sb", [NE, NA], F32) as eaw_sb, \
         nc.sbuf_tensor("eawb", [NE, NA], BF16) as eawb, \
         nc.sbuf_tensor("erow", [NE, 4], F32) as erow, \
         nc.psum_tensor("ps_e", [NA, NE], BF16) as ps_e:
        g0, v0, c0, t0 = cnt["gd"], cnt["v"], cnt["c"], cnt["t"]

        @b.gpsimd
        def _(e):
            inc("gd", e.dma_start(out=eaw_sb[:, :], in_=eaw_d[:, :]))
            e.wait_ge(sems["gd"], cnt["gd"])

        @b.vector
        def _(e):
            e.wait_ge(sems["gd"], g0 + 16)
            inc("v", e.reduce_max(erow[:, 0:1], eaw_sb[:, :], axis=AX))
            e.wait_ge(sems["v"], v0 + 1)
            e.wait_ge(sems["c"], c0 + 2)
            inc("v", e.reduce_sum(erow[:, 2:3], eaw_sb[:, :], axis=AX))
            e.wait_ge(sems["v"], v0 + 2)
            inc("v", e.reciprocal(erow[:, 3:4], erow[:, 2:3]))
            e.wait_ge(sems["v"], v0 + 3)
            e.wait_ge(sems["t"], t0 + 1)
            inc("v", e.tensor_copy(eawT[:, :], ps_e[:, :]))
            e.wait_ge(sems["v"], v0 + 4)

        @b.scalar
        def _(e):
            e.wait_ge(sems["v"], v0 + 1)
            inc("c", e.mul(erow[:, 1:2], erow[:, 0:1], -1.0))
            e.wait_ge(sems["c"], c0 + 1)
            inc("c", e.activation(eaw_sb[:, :], eaw_sb[:, :], AF.Exp,
                                  bias=erow[:, 1:2], scale=1.0))
            e.wait_ge(sems["c"], c0 + 2)
            e.wait_ge(sems["v"], v0 + 3)
            inc("c", e.activation(eawb[:, :], eaw_sb[:, :], AF.Copy,
                                  bias=0.0, scale=erow[:, 3:4]))
            e.wait_ge(sems["c"], c0 + 3)

        @b.tensor
        def _(e):
            e.wait_ge(sems["c"], c0 + 3)
            inc("t", e.transpose(ps_e[:, :], eawb[:, :],
                                 ident_b[0:NE, 0:NE]))

    with nc.Block() as b, \
         nc.sbuf_tensor("atoms_ch", [NA, 4096], BF16) as atoms_ch, \
         nc.sbuf_tensor("imp_ch", [NE, 4096], F32) as imp_ch, \
         nc.sbuf_tensor("wtmp", [NE, 4096], BF16) as wtmp, \
         nc.psum_tensor("ps_d", [NE, 2048], F32) as ps_d:
        ndc = (nslot_pad + 2047) // 2048
        g0, v0, c0, t0 = cnt["gd"], cnt["v"], cnt["c"], cnt["t"]

        @b.gpsimd
        def _(e):
            for i in range(ndc):
                n0 = i * 2048
                pn = (i % 2) * 2048
                if i >= 2:
                    e.wait_ge(sems["v"], v0 + (i - 1))
                if n0 < nslot:
                    n1 = min(nslot, n0 + 2048)
                    inc("gd", e.dma_start(out=atoms_ch[:, pn:pn + n1 - n0],
                                          in_=atoms_d[:, n0:n1]))
                    inc("gd", e.dma_start(out=imp_ch[:, pn:pn + n1 - n0],
                                          in_=imp_d[:, n0:n1]))
                else:
                    inc("gd", e.dma_start(out=atoms_ch[:, pn:pn + 2048],
                                          in_=atoms_d[:, 0:2048]))
                    inc("gd", e.dma_start(out=imp_ch[:, pn:pn + 2048],
                                          in_=imp_d[:, 0:2048]))
                e.wait_ge(sems["gd"], cnt["gd"])
                if i >= 1:
                    e.wait_ge(sems["v"], v0 + i)
                    inc("gd", e.dma_start(
                        out=weighted_hbm[:, (i - 1) * 2048:i * 2048],
                        in_=wtmp[:, ((i - 1) % 2) * 2048:((i - 1) % 2) * 2048 + 2048]))
                    e.wait_ge(sems["gd"], cnt["gd"])
            e.wait_ge(sems["v"], v0 + ndc)
            inc("gd", e.dma_start(
                out=weighted_hbm[:, (ndc - 1) * 2048:ndc * 2048],
                in_=wtmp[:, ((ndc - 1) % 2) * 2048:((ndc - 1) % 2) * 2048 + 2048]))
            e.wait_ge(sems["gd"], cnt["gd"])

        @b.scalar
        def _(e):
            for i in range(ndc):
                pn = (i % 2) * 2048
                e.wait_ge(sems["gd"], g0 + 32 * (i + 1) + 16 * max(0, i))
                inc("c", e.activation(imp_ch[:, pn:pn + 2048],
                                      imp_ch[:, pn:pn + 2048], AF.Sigmoid,
                                      bias=0.0, scale=1.0))
                e.wait_ge(sems["c"], cnt["c"])

        @b.tensor
        def _(e):
            for i in range(ndc):
                pn = (i % 2) * 2048
                e.wait_ge(sems["gd"], g0 + 32 * (i + 1) + 16 * max(0, i))
                if i >= 1:
                    e.wait_ge(sems["v"], v0 + i)
                mm = None
                for q4 in range(4):
                    mm = e.matmul(ps_d[:, q4 * 512:(q4 + 1) * 512], eawT[:, :],
                                  atoms_ch[:, pn + q4 * 512:pn + (q4 + 1) * 512],
                                  start=True, stop=True)
                inc("t", mm)

        @b.vector
        def _(e):
            for i in range(ndc):
                pn = (i % 2) * 2048
                e.wait_ge(sems["t"], t0 + i + 1)
                e.wait_ge(sems["c"], c0 + i + 1)
                inc("v", e.tensor_tensor(wtmp[:, pn:pn + 2048], ps_d[:, :],
                                         imp_ch[:, pn:pn + 2048], ALU.mult))
                e.wait_ge(sems["v"], cnt["v"])

    def residual_ln_pass(x_src, addT, alpha, x_dst, want_xn, want_hbm=False):
        with nc.Block() as b, \
             nc.psum_tensor(uname("ptA"), [128, H], BF16) as ptA, \
             nc.psum_tensor(uname("ptB"), [128, H], BF16) as ptB:
            v0, c0, g0, gc0 = cnt["v"], cnt["c"], cnt["gd"], cnt["g"]
            t0, sd0 = cnt["t"], cnt["sd"]
            e0c = {nm: cnt[nm] for nm in ("e0", "e1")}
            store_x = addT is not None
            fAb = [fA, fB]
            stage = [sC, sD]
            pt = [ptA, ptB]
            ptr = [t.reshape([128, NT, 128]) for t in pt]
            XNT4 = XNT.reshape([128, NT, NT, 128])
            esem = ["e0", "e1"]

            def xnT_slice(j):
                return bass.AP(xnT_hbm, j * 128,
                               [[TC, 128], [128 * TC, NT], [1, 128]])

            def eload(e, j, src):
                inc(esem[j % 2], e.dma_start(
                    out=fAb[j % 2][:, :], in_=src[j * 128:(j + 1) * 128, :]))

            def ethr(j):
                return e0c[esem[j % 2]] + 16 * (j // 2 + 1)

            v_combine = {}
            v_reduce = {}
            v_rm1 = {}
            v_recip = {}
            v_pcopy = {}
            c_square = {}
            c_sqrt = {}
            c_apply = {}
            g_pcopy = {}
            v = v0
            for j in range(NT):
                if addT is not None:
                    v += 1
                    v_combine[j] = v
                if want_xn:
                    v += 1
                    v_reduce[j] = v
                    v += 6
                    v_rm1[j] = v
                    if j >= 4 and (j - 3) % 2 == 1:
                        v += 1
                        v_pcopy[j - 3] = v
            if want_xn:
                for k in (NT - 3, NT - 1):
                    v += 1
                    v_pcopy[k] = v
            c = c0
            g = gc0
            if want_xn:
                for j in range(NT):
                    c += 1
                    c_square[j] = c
                    if j >= 1:
                        c += 3
                        c_apply[j - 1] = c
                    if j >= 3 and (j - 3) % 2 == 0:
                        g += 1
                        g_pcopy[j - 3] = g
                c += 3
                c_apply[NT - 1] = c
                g += 1
                g_pcopy[NT - 2] = g

            def t_trans(j):
                return t0 + j + 1

            def w_pcopy(e, j):
                if j % 2 == 0:
                    e.wait_ge(sems["g"], g_pcopy[j])
                else:
                    e.wait_ge(sems["v"], v_pcopy[j])

            sd_at = {}
            sd_store = {}
            sdv = sd0
            sync_ops = []
            for j in range(NT):
                if addT is not None:
                    sdv += 16
                    sd_at[j] = sdv
                    sync_ops.append(("at", j))
                if want_hbm and j >= 4:
                    sdv += 16
                    sd_store[j - 4] = sdv
                    sync_ops.append(("st", j - 4))
            if want_hbm:
                for k in range(NT - 4, NT):
                    sdv += 16
                    sd_store[k] = sdv
                    sync_ops.append(("st", k))

            if sync_ops:
                @b.sync
                def _(e):
                    for kind, j in sync_ops:
                        if kind == "at":
                            if j >= 2:
                                e.wait_ge(sems["v"], v_combine[j - 2])
                            inc("sd", e.dma_start(
                                out=(sA if j % 2 == 0 else sB)[:, :],
                                in_=addT[:, j * 128:(j + 1) * 128],
                                transpose=True), 16)
                        else:
                            w_pcopy(e, j)
                            inc("sd", e.dma_start(
                                out=xnT_slice(j),
                                in_=XNT4[:, :, j, :]), 16)
                        e.wait_ge(sems["sd"], cnt["sd"])

            def emit_napply(e, j):
                e.wait_ge(sems["v"], v_rm1[j])
                inc("c", e.activation(statT[:, j:j + 1], statI[:, j:j + 1],
                                      AF.Sqrt, bias=zerov[:, 0:1], scale=1.0))
                e.wait_ge(sems["c"], cnt["c"])
                inc("c", e.activation(statI[:, j:j + 1], statM[:, j:j + 1],
                                      AF.Identity, bias=zerov[:, 0:1],
                                      scale=statT[:, j:j + 1]))
                e.wait_ge(sems["c"], cnt["c"])
                if j >= 2:
                    e.wait_ge(sems["t"], t_trans(j - 2))
                inc("c", e.activation(stage[j % 2][:, :], fAb[j % 2][:, :],
                                      AF.Identity, bias=statI[:, j:j + 1],
                                      scale=statT[:, j:j + 1]))
                e.wait_ge(sems["c"], cnt["c"])

            def emit_pcopy_c(e, j):
                e.wait_ge(sems["t"], t_trans(j))
                inc("g", e.activation(XNT4[:, :, j, :],
                                      ptr[0][:, :, :], AF.Copy,
                                      bias=0.0, scale=1.0))
                e.wait_ge(sems["g"], cnt["g"])

            @b.scalar
            def _(e):
                for j in range(2):
                    eload(e, j, x_src)
                for j in range(NT):
                    if want_xn:
                        if addT is not None:
                            e.wait_ge(sems["v"], v_combine[j])
                        else:
                            e.wait_ge(sems[esem[j % 2]], ethr(j))
                        inc("c", e.activation(sG[:, :], fAb[j % 2][:, :],
                                              AF.Square, bias=0.0, scale=1.0,
                                              accum_out=statQ[:, j:j + 1]))
                        e.wait_ge(sems["c"], cnt["c"])
                        if j >= 1:
                            emit_napply(e, j - 1)
                        if j >= 3 and (j - 3) % 2 == 0:
                            emit_pcopy_c(e, j - 3)
                        if 1 <= j < NT - 1:
                            e.wait_ge(sems["v"], v_reduce[j - 1])
                            if store_x:
                                e.wait_ge(sems["gd"], g0 + 16 * j)
                            eload(e, j + 1, x_src)
                    else:
                        if j + 2 < NT:
                            if addT is not None:
                                e.wait_ge(sems["v"], v_combine[j])
                            else:
                                e.wait_ge(sems[esem[j % 2]], ethr(j))
                            if store_x:
                                e.wait_ge(sems["gd"], g0 + 16 * (j + 1))
                            eload(e, j + 2, x_src)
                if want_xn:
                    emit_napply(e, NT - 1)
                    emit_pcopy_c(e, NT - 2)

            @b.vector
            def _(e):
                for j in range(NT):
                    e.wait_ge(sems[esem[j % 2]], ethr(j))
                    if addT is not None:
                        if j >= 2:
                            if want_xn:
                                e.wait_ge(sems["c"], c_apply[j - 2])
                            if store_x:
                                e.wait_ge(sems["gd"], g0 + 16 * (j - 1))
                        e.wait_ge(sems["sd"], sd_at[j])
                        inc("v", e.scalar_tensor_tensor(
                            fAb[j % 2][:, :],
                            (sA if j % 2 == 0 else sB)[:, :], float(alpha),
                            fAb[j % 2][:, :], ALU.mult, ALU.add))
                        e.wait_ge(sems["v"], cnt["v"])
                    if want_xn:
                        inc("v", e.reduce_sum(statS[:, j:j + 1],
                                              fAb[j % 2][:, :], axis=AX))
                        e.wait_ge(sems["v"], cnt["v"])
                        inc("v", e.tensor_scalar_mul(statM[:, j:j + 1],
                                                     statS[:, j:j + 1],
                                                     -1.0 / H))
                        e.wait_ge(sems["c"], c_square[j])
                        inc("v", e.tensor_scalar_mul(statV[:, j:j + 1],
                                                     statQ[:, j:j + 1],
                                                     1.0 / H))
                        e.wait_ge(sems["v"], cnt["v"])
                        inc("v", e.tensor_mul(statT[:, j:j + 1],
                                              statM[:, j:j + 1],
                                              statM[:, j:j + 1]))
                        e.wait_ge(sems["v"], cnt["v"])
                        inc("v", e.tensor_sub(statV[:, j:j + 1],
                                              statV[:, j:j + 1],
                                              statT[:, j:j + 1]))
                        e.wait_ge(sems["v"], cnt["v"])
                        inc("v", e.tensor_scalar_add(statV[:, j:j + 1],
                                                     statV[:, j:j + 1], EPS))
                        e.wait_ge(sems["v"], cnt["v"])
                        inc("v", e.reciprocal(statI[:, j:j + 1],
                                              statV[:, j:j + 1]))
                        e.wait_ge(sems["v"], cnt["v"])
                        if j >= 4 and (j - 3) % 2 == 1:
                            e.wait_ge(sems["t"], t_trans(j - 3))
                            inc("v", e.tensor_copy(XNT4[:, :, j - 3, :],
                                                   ptr[1][:, :, :]))
                            e.wait_ge(sems["v"], cnt["v"])
                if want_xn:
                    for k in (NT - 3, NT - 1):
                        e.wait_ge(sems["t"], t_trans(k))
                        inc("v", e.tensor_copy(XNT4[:, :, k, :],
                                               ptr[1][:, :, :]))
                        e.wait_ge(sems["v"], cnt["v"])

            if want_xn:
                @b.tensor
                def _(e):
                    for j in range(NT):
                        e.wait_ge(sems["c"], c_apply[j])
                        if j >= 2:
                            w_pcopy(e, j - 2)
                        tr = None
                        for k in range(NT):
                            tr = e.transpose(
                                pt[j % 2][:, k * 128:(k + 1) * 128],
                                stage[j % 2][:, k * 128:(k + 1) * 128],
                                ident_b[:, :])
                        inc("t", tr)

            @b.gpsimd
            def _(e):
                for j in range(NT):
                    if j % 2 == 0:
                        emit_casts(e, 1)
                    if store_x:
                        if addT is not None:
                            e.wait_ge(sems["v"], v_combine[j])
                        inc("gd", e.dma_start(
                            out=x_dst[j * 128:(j + 1) * 128, :],
                            in_=fAb[j % 2][:, :]))
                        e.wait_ge(sems["gd"], cnt["gd"])

    def matmul_phase(wdram_b, rhs_fn, evac_emit, cpi, wname=None):
        with nc.Block() as b, \
             nc.psum_tensor(uname("pA"), [128, 512], F32) as p0, \
             nc.psum_tensor(uname("pB"), [128, 512], F32) as p1, \
             nc.psum_tensor(uname("pC"), [128, 512], F32) as p2, \
             nc.psum_tensor(uname("pD"), [128, 512], F32) as p3, \
             nc.psum_tensor(uname("pE"), [128, 512], F32) as p4, \
             nc.psum_tensor(uname("pF"), [128, 512], F32) as p5, \
             nc.psum_tensor(uname("pG"), [128, 512], F32) as p6, \
             nc.psum_tensor(uname("pH"), [128, 512], F32) as p7:
            ps = [p0, p1, p2, p3, p4, p5, p6, p7]
            bases = {k: cnt[k] for k in cnt}

            wsem = cast_done.get(wname) if wname else None

            @b.sync
            def _(e):
                for m in range(NT):
                    if wsem:
                        e.wait_ge(sems[wsem], 128)
                    if m >= 2:
                        e.wait_ge(sems["t"], bases["t"] + (m - 1))
                    inc("sd", e.dma_start(out=wslab[m % 2][:, :],
                                          in_=wdram_b[m, :, :]), 16)
                    e.wait_ge(sems["sd"], cnt["sd"])

            @b.tensor
            def _(e):
                for m in range(NT):
                    e.wait_ge(sems["sd"], bases["sd"] + 16 * (m + 1))
                    if m >= 2:
                        e.wait_ge(sems["c"], bases["c"] + cpi * (m - 1))
                    mm = None
                    for k in range(NT):
                        for n in range(4):
                            mm = e.matmul(ps[4 * (m % 2) + n][:, :],
                                          wslab[m % 2][:, k * 128:(k + 1) * 128],
                                          rhs_fn(k, n),
                                          start=(k == 0), stop=(k == NT - 1))
                    inc("t", mm)

            evac_emit(b, ps, bases)
        return

    def evac_store(out_hbm, func, bias_name, elu):
        cpi = 5 if elu else 4

        def emit(b, ps, bases):
            tile = [sE, sF]

            @b.scalar
            def _(e):
                for m in range(NT):
                    e.wait_ge(sems["t"], bases["t"] + m + 1)
                    if m >= 2:
                        e.wait_ge(sems["gd"], bases["gd"] + 16 * (m - 1))
                    bias = bias_sb[bias_name][:, m:m + 1] if bias_name else 0.0
                    fn = func
                    if bias_name and fn == AF.Copy:
                        fn = AF.Identity
                    for n in range(4):
                        inc("c", e.activation(
                            tile[m % 2][:, n * 512:(n + 1) * 512],
                            ps[4 * (m % 2) + n][:, :], fn, bias=bias, scale=1.0))
                    if elu:
                        e.wait_ge(sems["v"], bases["v"] + 3 * m + 1)
                        inc("c", e.activation(sD[:, :], sC[:, :], AF.Exp,
                                              bias=0.0, scale=1.0))
                    e.wait_ge(sems["c"], cnt["c"])

            if elu:
                @b.vector
                def _(e):
                    for m in range(NT):
                        e.wait_ge(sems["c"], bases["c"] + cpi * m + 4)
                        inc("v", e.tensor_scalar_min(sC[:, :],
                                                     tile[m % 2][:, :], 0.0))
                        e.wait_ge(sems["v"], cnt["v"])
                        inc("v", e.tensor_scalar_max(sG[:, :],
                                                     tile[m % 2][:, :], 0.0))
                        e.wait_ge(sems["v"], cnt["v"])
                        e.wait_ge(sems["c"], bases["c"] + cpi * (m + 1))
                        inc("v", e.scalar_tensor_tensor(
                            tile[m % 2][:, :], sD[:, :], -1.0, sG[:, :],
                            ALU.add, ALU.add))
                        e.wait_ge(sems["v"], cnt["v"])

            @b.gpsimd
            def _(e):
                for m in range(NT):
                    if m % 4 == 0:
                        emit_casts(e, 1)
                    if elu:
                        e.wait_ge(sems["v"], bases["v"] + 3 * (m + 1))
                    else:
                        e.wait_ge(sems["c"], bases["c"] + cpi * (m + 1))
                    inc("gd", e.dma_start(
                        out=out_hbm[m * 128:(m + 1) * 128, :],
                        in_=tile[m % 2][:, :]))
                    e.wait_ge(sems["gd"], cnt["gd"])
        return emit, cpi

    def ffn(x_src, x_dst, next_want_xn):
        with nc.Block() as b, \
             nc.psum_tensor(uname("ps_l"), [NE, 512], F32) as ps_l:
            t0, v0 = cnt["t"], cnt["v"]

            @b.tensor
            def _(e):
                for n in range(4):
                    if n >= 1:
                        e.wait_ge(sems["v"], v0 + n)
                    mm = None
                    for k in range(NT):
                        mm = e.matmul(ps_l[:, :], gwt[:, k * NE:(k + 1) * NE],
                                      xnt_rhs(k, n),
                                      start=(k == 0), stop=(k == NT - 1))
                    inc("t", mm)

            @b.vector
            def _(e):
                for n in range(4):
                    e.wait_ge(sems["t"], t0 + n + 1)
                    inc("v", e.tensor_scalar(
                        fA[0:NE, n * 512:(n + 1) * 512], ps_l[:, :],
                        50.0, -50.0, ALU.min, ALU.max))
                    e.wait_ge(sems["v"], cnt["v"])

        with nc.Block() as b, \
             nc.psum_tensor(uname("ps_lt"), [128, NE], F32) as ps_lt:
            t0, v0 = cnt["t"], cnt["v"]

            @b.tensor
            def _(e):
                for j in range(NT):
                    if j >= 1:
                        e.wait_ge(sems["v"], v0 + j)
                    inc("t", e.transpose(ps_lt[:, :],
                                         fA[0:NE, j * 128:(j + 1) * 128],
                                         ident_f[0:NE, 0:NE]))

            @b.vector
            def _(e):
                for j in range(NT):
                    e.wait_ge(sems["t"], t0 + j + 1)
                    inc("v", e.tensor_copy(lg_tm[:, j * NE:(j + 1) * NE],
                                           ps_lt[:, :]))
                    e.wait_ge(sems["v"], cnt["v"])

        with nc.Block() as b, \
             nc.psum_tensor(uname("ps_pt"), [NE, 128], BF16) as ps_pt:
            t0, v0, c0 = cnt["t"], cnt["v"], cnt["c"]

            @b.vector
            def _(e):
                def lv(inst):
                    inc("v", inst)
                    e.wait_ge(sems["v"], cnt["v"])
                lv(e.reduce_max(m1c[:, :], g3(lg_tm), axis=AX))
                lv(e.tensor_tensor(g3(eq1), g3(lg_tm), g3b(m1c), ALU.is_equal))
                lv(e.scalar_tensor_tensor(g3(l2m), g3(eq1), -1e30, g3(lg_tm),
                                          ALU.mult, ALU.add))
                lv(e.reduce_max(m2c[:, :], g3(l2m), axis=AX))
                lv(e.tensor_tensor(g3(eq2), g3(l2m), g3b(m2c), ALU.is_equal))
                lv(e.tensor_sub(dc[:, :], m2c[:, :], m1c[:, :]))
                e.wait_ge(sems["c"], c0 + 1)
                lv(e.tensor_scalar_add(rw1c[:, :], dc[:, :], 1.0))
                lv(e.reciprocal(rw1c[:, :], rw1c[:, :]))
                lv(e.tensor_mul(rw2c[:, :], dc[:, :], rw1c[:, :]))
                lv(e.tensor_tensor(g3(p_tm), g3(eq1), g3b(rw1c), ALU.mult))
                lv(e.tensor_tensor(g3(eq2), g3(eq2), g3b(rw2c), ALU.mult))
                lv(e.tensor_add(p_tm[:, :], p_tm[:, :], eq2[:, :]))
                lv(e.tensor_copy(pb_tm[:, :], p_tm[:, :]))

            @b.scalar
            def _(e):
                e.wait_ge(sems["v"], v0 + 6)
                inc("c", e.activation(dc[:, :], dc[:, :], AF.Exp,
                                      bias=0.0, scale=1.0))
                e.wait_ge(sems["c"], cnt["c"])

            @b.tensor
            def _(e):
                e.wait_ge(sems["v"], v0 + 13)
                for j in range(NT):
                    if j >= 1:
                        e.wait_ge(sems["v"], v0 + 13 + j)
                    inc("t", e.transpose(ps_pt[:, :],
                                         pb_tm[:, j * NE:(j + 1) * NE],
                                         ident_b[:, :]))

            @b.vector
            def _(e):
                for j in range(NT):
                    e.wait_ge(sems["t"], t0 + j + 1)
                    inc("v", e.tensor_copy(pT[:, j * 128:(j + 1) * 128],
                                           ps_pt[:, :]))
                    e.wait_ge(sems["v"], cnt["v"])

        with nc.Block() as b, \
             nc.psum_tensor(uname("q0"), [128, 2048], F32) as q0_, \
             nc.psum_tensor(uname("q1"), [128, 2048], F32) as q1_:
            qs = [q0_, q1_]
            xg = [sA, sB, sX, sY]
            xsem = ["x0", "x1", "x2", "x3"]
            vt = [sC, sD]
            acc = [sE, sF]
            t0, v0, c0, s0, sd0 = (cnt["t"], cnt["v"], cnt["c"], cnt["gd"],
                                   cnt["sd"])
            x0c = {nm: cnt[nm] for nm in xsem}
            sched = []
            spt_of_pt = {}
            for pt in range(NT):
                if tiles_j[pt] == 0:
                    continue
                spt = len(spt_of_pt)
                spt_of_pt[pt] = spt
                for j in range(tiles_j[pt]):
                    sched.append((spt, pt, j))
            assert len(sched) == nst
            last_st = {}
            for i, (spt, pt, j) in enumerate(sched):
                last_st[spt] = i
            inv_last = {v: k for k, v in last_st.items()}
            pt_of_spt = {v: k for k, v in spt_of_pt.items()}
            ngrp_l = (nst + 15) // 16

            @b.sync
            def _(e):
                for g in range(ngrp_l):
                    if g >= 2:
                        e.wait_ge(sems["t"], t0 + 16 * (g - 1))
                    inc("sd", e.dma_start(
                        out=wchunk[g % 2][:, :],
                        in_=weighted_hbm[:, g * 2048:(g + 1) * 2048]), 16)
                    e.wait_ge(sems["sd"], cnt["sd"])

            @b.tensor
            def _(e):
                for st in range(nst):
                    g = st // 16
                    e.wait_ge(sems["sd"], sd0 + 16 * (g + 1))
                    if st >= 2:
                        e.wait_ge(sems["c"], c0 + (st - 1))
                    mm = None
                    for n in range(4):
                        mm = e.matmul(qs[st % 2][:, n * 512:(n + 1) * 512],
                                      wchunk[g % 2][:, (st % 16) * 128:
                                                    (st % 16 + 1) * 128],
                                      pT[:, n * 512:(n + 1) * 512],
                                      start=True, stop=True)
                    inc("t", mm)

            @b.scalar
            def _(e):
                for st, (spt, pt, j) in enumerate(sched):
                    e.wait_ge(sems["t"], t0 + st + 1)
                    if st >= 2:
                        e.wait_ge(sems["v"], v0 + 2 * (st - 1))
                    inc("c", e.activation(
                        vt[st % 2][:, :], qs[st % 2][:, :], AF.Copy,
                        bias=0.0, scale=1.0))
                    e.wait_ge(sems["c"], cnt["c"])

            @b.gpsimd
            def _(e):
                for st in range(nst):
                    if st % 12 == 0:
                        emit_casts(e, 1)
                    if st >= 4:
                        e.wait_ge(sems["v"], v0 + 2 * (st - 3))
                    inc(xsem[st % 4], e.indirect_dma_start(
                        xg[st % 4][:, :], None, xnT_hbm[:, :],
                        IndirectOffsetOnAxis(colidx[:, st:st + 1], 0)), 16)
                    if st in inv_last:
                        spt_l = inv_last[st]
                        e.wait_ge(sems["v"], v0 + 2 * (st + 1))
                        inc("gd", e.dma_start(
                            out=contrib_hbm[pt_of_spt[spt_l] * 128:
                                            (pt_of_spt[spt_l] + 1) * 128, :],
                            in_=acc[spt_l % 2][:, :]))
                        e.wait_ge(sems["gd"], cnt["gd"])

            @b.vector
            def _(e):
                for st, (spt, pt, j) in enumerate(sched):
                    e.wait_ge(sems[xsem[st % 4]],
                              x0c[xsem[st % 4]] + 16 * (st // 4 + 1))
                    e.wait_ge(sems["c"], c0 + (st + 1))
                    if j == 0:
                        if spt >= 2:
                            e.wait_ge(sems["gd"], s0 + 16 * (spt - 1))
                        inc("v", e.tensor_mul(acc[spt % 2][:, :],
                                              xg[st % 4][:, :],
                                              vt[st % 2][:, :]))
                        inc("v", e.engine_nop())
                        e.wait_ge(sems["v"], cnt["v"])
                    else:
                        inc("v", e.tensor_mul(wslab[st % 2][:, :],
                                              xg[st % 4][:, :],
                                              vt[st % 2][:, :]))
                        e.wait_ge(sems["v"], cnt["v"])
                        inc("v", e.tensor_add(acc[spt % 2][:, :],
                                              acc[spt % 2][:, :],
                                              wslab[st % 2][:, :]))
                        e.wait_ge(sems["v"], cnt["v"])


        def evac_w(b, ps, bases):
            tile = [sE, sF]
            cbuf = [sC, sD]

            @b.scalar
            def _(e):
                for m in range(NT):
                    e.wait_ge(sems["t"], bases["t"] + m + 1)
                    for n in range(4):
                        inc("c", e.activation(
                            tile[m % 2][:, n * 512:(n + 1) * 512],
                            ps[4 * (m % 2) + n][:, :], AF.Copy, bias=0.0, scale=1.0))
                    e.wait_ge(sems["c"], cnt["c"])
                    e.wait_ge(sems["v"], bases["v"] + (m + 1))
                    inc("se", e.dma_start(
                        out=outT_hbm[m * 128:(m + 1) * 128, :],
                        in_=tile[m % 2][:, :]))
                    e.wait_ge(sems["se"], cnt["se"])

            @b.gpsimd
            def _(e):
                for m in range(NT):
                    if m % 4 == 0:
                        emit_casts(e, 1)
                    if m >= 2:
                        e.wait_ge(sems["v"], bases["v"] + (m - 1))
                    inc("gd", e.indirect_dma_start(
                        cbuf[m % 2][:, :], None, contrib_hbm[:, :],
                        IndirectOffsetOnAxis(sigma_sb[:, m:m + 1], 0)))
                    e.wait_ge(sems["gd"], cnt["gd"])

            @b.vector
            def _(e):
                for m in range(NT):
                    e.wait_ge(sems["gd"], bases["gd"] + 16 * (m + 1))
                    e.wait_ge(sems["c"], bases["c"] + 4 * (m + 1))
                    inc("v", e.tensor_add(tile[m % 2][:, :], tile[m % 2][:, :],
                                          cbuf[m % 2][:, :]))
                    e.wait_ge(sems["v"], cnt["v"])


        matmul_phase(wb["ww"], xnt_rhs, evac_w, cpi=4, wname="ww")
        residual_ln_pass(x_src, outT_hbm, 0.5, x_dst,
                         want_xn=next_want_xn, want_hbm=False)

    def attention(x_src, x_dst):
        em, cpi = evac_store(qT_hbm, AF.Copy, "bqk", elu=True)
        matmul_phase(wb["wq"], xnt_rhs, em, cpi, wname="wq")
        em, cpi = evac_store(vT_hbm, AF.Silu, "bv", elu=False)
        matmul_phase(wb["wv"], xnt_rhs, em, cpi, wname="wv")

        with nc.Block() as b, \
             nc.psum_tensor(uname("pkv"), [128, HEADS * 128], F32) as pkv, \
             nc.psum_tensor(uname("u0"), [128, 512], F32) as u0_, \
             nc.psum_tensor(uname("u1"), [128, 512], F32) as u1_, \
             nc.psum_tensor(uname("u2"), [128, 512], F32) as u2_, \
             nc.psum_tensor(uname("u3"), [128, 512], F32) as u3_:
            us_ = [u0_, u1_, u2_, u3_]
            utile = [sE, sF]
            s0, t0, v0, c0, g0 = (cnt["sd"], cnt["t"], cnt["v"], cnt["c"],
                                  cnt["gd"])
            wsem_u = cast_done.get("wu")
            kvinc = {}
            utinc = {}
            pos = 0
            units_ku = []
            for m in range(NT):
                if m < 8:
                    for jj in (2 * m, 2 * m + 1):
                        pos += 1
                        kvinc[jj] = pos
                        units_ku.append(("kv", jj))
                pos += 1
                utinc[m] = pos
                units_ku.append(("uT", m))
            sd_q = {}
            sd_s = {}
            run = s0
            loads = []
            for u in units_ku:
                if u[0] == "kv":
                    loads.append(("q", u[1]))
                    loads.append(("v", u[1]))
                else:
                    loads.append(("s", u[1]))
            for kind, idx in loads:
                run += 16
                if kind == "v":
                    sd_q[idx] = run
                elif kind == "s":
                    sd_s[idx] = run

            @b.sync
            def _(e):
                for kind, idx in loads:
                    if kind == "q":
                        if idx >= 2:
                            e.wait_ge(sems["t"], t0 + kvinc[idx - 2])
                        inc("sd", e.dma_start(
                            out=(sA if idx % 2 == 0 else sC)[:, :],
                            in_=qT_hbm[:, idx * 128:(idx + 1) * 128],
                            transpose=True), 16)
                    elif kind == "v":
                        inc("sd", e.dma_start(
                            out=(sB if idx % 2 == 0 else sD)[:, :],
                            in_=vT_hbm[:, idx * 128:(idx + 1) * 128],
                            transpose=True), 16)
                    else:
                        if wsem_u:
                            e.wait_ge(sems[wsem_u], 128)
                        if idx >= 2:
                            e.wait_ge(sems["t"], t0 + utinc[idx - 2])
                        inc("sd", e.dma_start(out=wslab[idx % 2][:, :],
                                              in_=wb["wu"][idx, :, :]), 16)
                    e.wait_ge(sems["sd"], cnt["sd"])

            @b.tensor
            def _(e):
                for u in units_ku:
                    if u[0] == "kv":
                        j = u[1]
                        e.wait_ge(sems["sd"], sd_q[j])
                        if j >= 1:
                            e.wait_ge(sems["v"], v0 + j)
                        qa = sA if j % 2 == 0 else sC
                        va = sB if j % 2 == 0 else sD
                        mm = None
                        for h in range(HEADS):
                            mm = e.matmul(pkv[:, h * 128:(h + 1) * 128],
                                          qa[:, h * 128:(h + 1) * 128],
                                          va[:, h * 128:(h + 1) * 128],
                                          start=True, stop=True)
                        inc("t", mm)
                    else:
                        m = u[1]
                        e.wait_ge(sems["sd"], sd_s[m])
                        if m >= 1:
                            e.wait_ge(sems["c"], c0 + 4 * m)
                        mm = None
                        for k in range(NT):
                            for n in range(4):
                                mm = e.matmul(
                                    us_[n][:, :],
                                    wslab[m % 2][:, k * 128:(k + 1) * 128],
                                    xnt_rhs(k, n),
                                    start=(k == 0), stop=(k == NT - 1))
                        inc("t", mm)

            @b.vector
            def _(e):
                for j in range(NT):
                    e.wait_ge(sems["t"], t0 + kvinc[j])
                    if j == 0:
                        inc("v", e.tensor_copy(kv_sb[:, :], pkv[:, :]))
                    else:
                        inc("v", e.tensor_add(kv_sb[:, :], kv_sb[:, :],
                                              pkv[:, :]))
                    e.wait_ge(sems["v"], cnt["v"])

            @b.scalar
            def _(e):
                for m in range(NT):
                    e.wait_ge(sems["t"], t0 + utinc[m])
                    if m >= 2:
                        e.wait_ge(sems["gd"], g0 + 16 * (m - 1) +
                                  (16 if m - 2 >= 8 else 0))
                    for n in range(4):
                        inc("c", e.activation(
                            utile[m % 2][:, n * 512:(n + 1) * 512],
                            us_[n][:, :], AF.Silu,
                            bias=bias_sb["bu"][:, m:m + 1], scale=1.0))
                    e.wait_ge(sems["c"], cnt["c"])

            @b.gpsimd
            def _(e):
                for m in range(NT):
                    if m % 4 == 0:
                        emit_casts(e, 1)
                    e.wait_ge(sems["c"], c0 + 4 * (m + 1))
                    inc("gd", e.dma_start(
                        out=uT_hbm[m * 128:(m + 1) * 128, :],
                        in_=utile[m % 2][:, :]))
                    e.wait_ge(sems["gd"], cnt["gd"])
                    if m == 8:
                        e.wait_ge(sems["v"], v0 + NT)
                        inc("gd", e.dma_start(out=cc_in[:, :], in_=kv_sb[:, :]))
                        e.wait_ge(sems["gd"], cnt["gd"])
                        inc("cc", e.collective_compute(
                            "AllReduce", ALU.add,
                            replica_groups=[[0, 1, 2, 3], [4, 5, 6, 7]],
                            ins=[cc_in.ap().opt()],
                            outs=[cc_out.ap().opt()]))
        with nc.Block() as b, \
             nc.psum_tensor(uname("o0"), [128, 512], F32) as o0_, \
             nc.psum_tensor(uname("o1"), [128, 512], F32) as o1_, \
             nc.psum_tensor(uname("o2"), [128, 512], F32) as o2_, \
             nc.psum_tensor(uname("o3"), [128, 512], F32) as o3_:
            os_ = [o0_, o1_, o2_, o3_]
            s0, t0, c0, v0, g0 = (cnt["sd"], cnt["t"], cnt["c"], cnt["v"],
                                  cnt["gd"])

            @b.gpsimd
            def _(e):
                e.wait_ge(sems["cc"], cnt["cc"])
                inc("gd", e.dma_start(out=kv_sb[:, :], in_=cc_out[:, :]))
                e.wait_ge(sems["gd"], cnt["gd"])

            @b.vector
            def _(e):
                e.wait_ge(sems["gd"], g0 + 16)
                inc("v", e.tensor_copy(kvb[:, :], kv_sb[:, :]))
                e.wait_ge(sems["v"], cnt["v"])

            @b.sync
            def _(e):
                for h in range(HEADS):
                    if h >= 2:
                        e.wait_ge(sems["t"], t0 + (h - 1))
                    inc("sd", e.dma_start(
                        out=wslab[h % 2][:, :],
                        in_=qT_hbm[h * 128:(h + 1) * 128, :]), 16)
                    e.wait_ge(sems["sd"], cnt["sd"])

            @b.tensor
            def _(e):
                e.wait_ge(sems["v"], v0 + 1)
                for h in range(HEADS):
                    e.wait_ge(sems["sd"], s0 + 16 * (h + 1))
                    if h >= 1:
                        e.wait_ge(sems["c"], c0 + 4 * h)
                    mm = None
                    for n in range(4):
                        mm = e.matmul(os_[n][:, :],
                                      kvb[:, h * 128:(h + 1) * 128],
                                      wslab[h % 2][:, n * 512:(n + 1) * 512],
                                      start=True, stop=True)
                    inc("t", mm)

            @b.scalar
            def _(e):
                for h in range(HEADS):
                    e.wait_ge(sems["t"], t0 + h + 1)
                    for n in range(4):
                        inc("c", e.activation(
                            XNT[:, h * TC + n * 512:h * TC + (n + 1) * 512],
                            os_[n][:, :], AF.Copy, bias=0.0, scale=1.0))
                    e.wait_ge(sems["c"], cnt["c"])

        with nc.Block() as b, \
             nc.psum_tensor(uname("ps_r"), [1, TC], F32) as ps_r, \
             nc.psum_tensor(uname("ps_r2"), [1, TC], F32) as ps_r2:
            t0, c0, v0 = cnt["t"], cnt["c"], cnt["v"]

            @b.scalar
            def _(e):
                for h in range(HEADS):
                    if h >= 1:
                        e.wait_ge(sems["t"], t0 + h)
                    inc("c", e.activation(sG[:, :], XNT[:, h * TC:(h + 1) * TC],
                                          AF.Square, bias=0.0, scale=1.0))
                    e.wait_ge(sems["c"], cnt["c"])
                e.wait_ge(sems["v"], v0 + 4)
                inc("c", e.activation(row_t[0:1, 0:TC], fB[0:1, 0:TC], AF.Sqrt,
                                      bias=epsv[0:1, 0:1], scale=1.0))
                e.wait_ge(sems["c"], cnt["c"])

            @b.tensor
            def _(e):
                for h in range(HEADS):
                    e.wait_ge(sems["c"], c0 + h + 1)
                    mm = None
                    for n in range(4):
                        mm = e.matmul(ps_r[:, n * 512:(n + 1) * 512],
                                      ones_k[:, :],
                                      XNT[:, h * TC + n * 512:
                                          h * TC + (n + 1) * 512],
                                      start=(h == 0), stop=(h == HEADS - 1))
                    for n in range(4):
                        mm = e.matmul(ps_r2[:, n * 512:(n + 1) * 512],
                                      ones_k[:, :],
                                      sG[:, n * 512:(n + 1) * 512],
                                      start=(h == 0), stop=(h == HEADS - 1))
                    inc("t", mm)

            @b.vector
            def _(e):
                e.wait_ge(sems["t"], t0 + HEADS)
                inc("v", e.tensor_scalar_mul(fA[0:1, 0:TC], ps_r[:, :], 1.0 / H))
                inc("v", e.tensor_scalar_mul(fB[0:1, 0:TC], ps_r2[:, :], 1.0 / H))
                e.wait_ge(sems["v"], cnt["v"])
                inc("v", e.tensor_mul(row_t[0:1, 0:TC], fA[0:1, 0:TC],
                                      fA[0:1, 0:TC]))
                e.wait_ge(sems["v"], cnt["v"])
                inc("v", e.tensor_sub(fB[0:1, 0:TC], fB[0:1, 0:TC],
                                      row_t[0:1, 0:TC]))
                e.wait_ge(sems["v"], cnt["v"])
                e.wait_ge(sems["c"], c0 + HEADS + 1)
                inc("v", e.reciprocal(fB[0:1, 0:TC], row_t[0:1, 0:TC]))
                e.wait_ge(sems["v"], cnt["v"])
                inc("v", e.tensor_mul(row_t[0:1, 0:TC], fA[0:1, 0:TC],
                                      fB[0:1, 0:TC]))
                e.wait_ge(sems["v"], cnt["v"])
                inc("v", e.tensor_scalar_mul(row_t[0:1, 0:TC],
                                             row_t[0:1, 0:TC], -1.0))
                e.wait_ge(sems["v"], cnt["v"])
                inc("v", e.tensor_copy(rowba[:, :], fB[0:1, 0:TC]))
                inc("v", e.tensor_copy(rowbb[:, :], row_t[0:1, 0:TC]))
                e.wait_ge(sems["v"], cnt["v"])

        with nc.Block() as b, \
             nc.psum_tensor(uname("pb0"), [128, 512], F32) as pb0, \
             nc.psum_tensor(uname("pb1"), [128, 512], F32) as pb1:
            t0, v0, c0, s0 = cnt["t"], cnt["v"], cnt["c"], cnt["sd"]

            @b.tensor
            def _(e):
                for n in range(4):
                    if n >= 1:
                        e.wait_ge(sems["v"], v0 + 2 * n)
                    e.matmul(pb0[:, :], ones_1[:, :],
                             rowba[:, n * 512:(n + 1) * 512],
                             start=True, stop=True)
                    inc("t", e.matmul(pb1[:, :], ones_1[:, :],
                                      rowbb[:, n * 512:(n + 1) * 512],
                                      start=True, stop=True))

            @b.vector
            def _(e):
                for n in range(4):
                    e.wait_ge(sems["t"], t0 + n + 1)
                    inc("v", e.tensor_copy(sA[:, n * 512:(n + 1) * 512],
                                           pb0[:, :]))
                    e.wait_ge(sems["v"], cnt["v"])
                    inc("v", e.tensor_copy(sB[:, n * 512:(n + 1) * 512],
                                           pb1[:, :]))
                    e.wait_ge(sems["v"], cnt["v"])

            vb = v0 + 8

            @b.sync
            def _(e):
                for h in range(HEADS):
                    if h >= 2:
                        e.wait_ge(sems["v"], vb + 3 * (h - 1))
                    inc("sd", e.dma_start(
                        out=(sC if h % 2 == 0 else sD)[:, :],
                        in_=uT_hbm[h * 128:(h + 1) * 128, :]), 16)
                    e.wait_ge(sems["sd"], cnt["sd"])

            @b.vector
            def _(e):
                for h in range(HEADS):
                    e.wait_ge(sems["sd"], s0 + 16 * (h + 1))
                    inc("v", e.tensor_mul(sE[:, :], XNT[:, h * TC:(h + 1) * TC],
                                          sA[:, :]))
                    e.wait_ge(sems["v"], cnt["v"])
                    inc("v", e.tensor_add(sE[:, :], sE[:, :], sB[:, :]))
                    e.wait_ge(sems["v"], cnt["v"])
                    e.wait_ge(sems["c"], c0 + h + 1)
                    inc("v", e.tensor_mul(XNT[:, h * TC:(h + 1) * TC],
                                          sG[:, :],
                                          (sC if h % 2 == 0 else sD)[:, :]))
                    e.wait_ge(sems["v"], cnt["v"])

            @b.scalar
            def _(e):
                for h in range(HEADS):
                    e.wait_ge(sems["v"], vb + 3 * h + 2)
                    inc("c", e.activation(sG[:, :], sE[:, :], AF.Identity,
                                          bias=bias_sb["lnb"][:, h:h + 1],
                                          scale=bias_sb["lng"][:, h:h + 1]))
                    e.wait_ge(sems["c"], cnt["c"])

        em, cpi = evac_store(outT_hbm, AF.Copy, "bo", elu=False)
        matmul_phase(wb["wo"], xnt_rhs, em, cpi, wname="wo")
        residual_ln_pass(x_src, outT_hbm, 1.0, x_dst,
                         want_xn=True, want_hbm=True)

    residual_ln_pass(x_in, None, 0.0, x_a, want_xn=True, want_hbm=True)
    ffn(x_in, x_a, next_want_xn=True)
    attention(x_a, x_b)
    ffn(x_b, y_out, next_want_xn=False)

    with nc.Block() as b:
        @b.sync
        def _(e):
            e.wait_ge(sems["gd"], cnt["gd"])
            e.wait_ge(sems["sd"], cnt["sd"])
            e.wait_ge(sems["se"], cnt["se"])

    stack.close()
    return nc


def kernel(**inputs):
    x = np.asarray(inputs["L0_attn_x"], dtype=np.float32).reshape(-1, H)
    mask_idx = np.asarray(inputs["L0_ffn_mask_idx"]).astype(np.int64)
    plan = _plan_slots(mask_idx)
    nst = plan["nst"]
    nslot = nst * 128

    wq = _pack_lhsT(np.asarray(inputs["L0_attn_Wqk"], dtype=np.float32))
    wv = _pack_lhsT(np.asarray(inputs["L0_attn_Wv"], dtype=np.float32))
    wu = _pack_lhsT(np.asarray(inputs["L0_attn_Wu"], dtype=np.float32))
    wo = _pack_lhsT(np.asarray(inputs["L0_attn_Wo"], dtype=np.float32))
    ww = _pack_lhsT(np.ascontiguousarray(
        np.asarray(inputs["L0_ffn_W"], dtype=np.float32).T))

    gw = np.asarray(inputs["L0_ffn_gate_w"], dtype=np.float32)
    gwt = np.ascontiguousarray(
        gw.T.reshape(NT, 128, NE).transpose(1, 0, 2).reshape(128, NT * NE))

    def tile_cols(v):
        return np.ascontiguousarray(v.reshape(NT, 128).T).astype(np.float32)

    bqk = tile_cols(np.asarray(inputs["L0_attn_bqk"], dtype=np.float32))
    bv = tile_cols(np.asarray(inputs["L0_attn_bv"], dtype=np.float32))
    bu = tile_cols(np.asarray(inputs["L0_attn_bu"], dtype=np.float32))
    bo = tile_cols(np.asarray(inputs["L0_attn_bo"], dtype=np.float32))
    lng = tile_cols(np.asarray(inputs["L0_attn_ln_g"], dtype=np.float32))
    lnb = tile_cols(np.asarray(inputs["L0_attn_ln_b"], dtype=np.float32))

    atoms = np.asarray(inputs["L0_ffn_atoms"], dtype=np.float32)
    imp = np.asarray(inputs["L0_ffn_importance"], dtype=np.float32)
    slot_s = plan["slot_s"].reshape(-1)
    valid = slot_s >= 0
    atoms_s = np.zeros((NA, nslot), dtype=np.float32)
    atoms_s[:, valid] = atoms[:, slot_s[valid]]
    imp_s = np.zeros((NE, nslot), dtype=np.float32)
    imp_s[:, valid] = imp[:, slot_s[valid]]
    colidx = np.ascontiguousarray(plan["slot_c"].astype(np.int32).T)
    sigma = np.ascontiguousarray(
        plan["sigma"].astype(np.int32).reshape(NT, 128).T)

    eaw = np.asarray(inputs["L0_ffn_expert_atom_weights"], dtype=np.float32)
    ident_b = np.eye(128, dtype=np.float32).astype(ml_dtypes.bfloat16)
    ident_f = np.eye(128, dtype=np.float32)

    nc = build_program(plan["tiles_j"], nst)

    shared = {"wq": wq, "wv": wv, "wu": wu, "wo": wo, "ww": ww, "gwt": gwt,
              "bqk": bqk, "bv": bv, "bu": bu, "bo": bo, "lng": lng, "lnb": lnb,
              "eaw": eaw, "atoms_s": atoms_s, "imp_s": imp_s, "colidx": colidx,
              "sigma": sigma,
              "ident_b": ident_b, "ident_f": ident_f}
    in_maps = []
    for c in range(8):
        m = dict(shared)
        m["x"] = np.ascontiguousarray(x[c * TC:(c + 1) * TC])
        in_maps.append(m)

    res = run_bass_kernel_spmd(nc, in_maps, core_ids=list(range(8)))
    out = np.concatenate([res.results[c]["y"] for c in range(8)], axis=0)
    return out.reshape(2, 8192, H).astype(np.float32)
